# version 20
# baseline (speedup 1.0000x reference)
import contextlib
import numpy as np
import ml_dtypes
import concourse.bass as bass
import concourse.mybir as mybir
from concourse.bass_utils import run_bass_kernel_spmd

F32 = mybir.dt.float32
BF16 = mybir.dt.bfloat16
I32 = mybir.dt.int32
AF = mybir.ActivationFunctionType
ALU = mybir.AluOpType


class Tok:
    __slots__ = ("k", "v")
    def __init__(self, k, v):
        self.k = k
        self.v = v


class Buf:
    __slots__ = ("h", "w", "r", "name")
    def __init__(self, h, name=""):
        self.h = h
        self.name = name or getattr(h, "name", "")
        self.w = None
        self.r = []
    def __getitem__(self, idx):
        return self.h[idx]
    @property
    def ap(self):
        return self.h[:]


ENGINES = ("tensor", "vector", "scalar", "gpsimd", "sync")


class Prog:
    def __init__(self):
        self.items = {e: [] for e in ENGINES}
        self.counts = {}
        for e in ENGINES:
            self.counts[e] = 0
            self.counts["dma_" + e] = 0
        self.wm = {e: {} for e in ENGINES}
        self.pending = {"dma_" + e: [] for e in ENGINES}

    def _fence(self, key):
        plist = self.pending[key]
        if not plist:
            return
        total = self.counts[key]
        for t in plist:
            t.v = total
        plist.clear()
        eng = key[4:]
        self.items[eng].append(([(key, total)], None, None, 0))
        wm = self.wm[eng]
        if wm.get(key, 0) < total:
            wm[key] = total

    def op(self, eng, fn, reads=(), writes=(), dma=False):
        toks = []
        for b in reads:
            if b.w:
                toks.append(b.w)
        for b in writes:
            if b.w:
                toks.append(b.w)
            toks.extend(b.r)
        for t in toks:
            if t.v is None:
                self._fence(t.k)
        best = {}
        for t in toks:
            if t.v > best.get(t.k, -1):
                best[t.k] = t.v
        wm = self.wm[eng]
        waits = []
        for k in sorted(best):
            v = best[k]
            if wm.get(k, 0) < v:
                waits.append((k, v))
                wm[k] = v
        key = ("dma_" + eng) if dma else eng
        step = 16 if dma else 1
        self.counts[key] += step
        if dma:
            tok = Tok(key, None)
            self.pending[key].append(tok)
        else:
            tok = Tok(key, self.counts[key])
        self.items[eng].append((waits, fn, key, step))
        for b in writes:
            b.w = tok
            b.r = []
        for b in reads:
            b.r.append(tok)
        return tok

    def barrier_tok(self, key):
        if key.startswith("dma_"):
            self._fence(key)
        return Tok(key, self.counts[key])

    def wait(self, eng, tok):
        if tok.v is None:
            self._fence(tok.k)
        wm = self.wm[eng]
        if wm.get(tok.k, 0) >= tok.v:
            return
        wm[tok.k] = tok.v
        self.items[eng].append(([(tok.k, tok.v)], None, None, 0))

    def emit(self, nc, es):
        used = set(k for lst in self.items.values()
                   for (ws, _f, _k, _s) in lst for (k, _v) in ws)
        used |= set(k for k, c in self.counts.items() if c > 0)
        sems = {k: es.enter_context(nc.semaphore("sq_" + k)) for k in sorted(used)}
        block = es.enter_context(nc.Block())
        for e in ENGINES:
            lst = self.items[e]
            if not lst:
                continue
            def make(lst):
                def body(h):
                    for (waits, fn, key, step) in lst:
                        for (k, v) in waits:
                            h.wait_ge(sems[k], v)
                        if fn is not None:
                            fn(h).then_inc(sems[key], step)
                return body
            getattr(block, e)(make(lst))



D = 1024
E = 2048
S = 64
RK = 64
HID = 3072
NL = 2
KD = D // 128
KE = E // 128
KH = HID // 128
EPS = 1e-5


def build_core(L=4096, TC=512):
    NT = L // TC
    nc = bass.Bass("TRN2", target_bir_lowering=False)

    xT = nc.dram_tensor("xT", [D, L], F32, kind="ExternalInput").ap()
    outT = nc.dram_tensor("outT", [D, L], F32, kind="ExternalOutput").ap()
    w_in = nc.dram_tensor("w_in", [NL, D, 2 * E], F32, kind="ExternalInput").ap()
    w_xp = nc.dram_tensor("w_xp", [NL, E, 192], F32, kind="ExternalInput").ap()
    w_dt = nc.dram_tensor("w_dt", [NL, RK, E], F32, kind="ExternalInput").ap()
    w_out = nc.dram_tensor("w_out", [NL, E, D], F32, kind="ExternalInput").ap()
    cw_r = nc.dram_tensor("cw_r", [NL, 128, KE * 4], F32, kind="ExternalInput").ap()
    cb_r = nc.dram_tensor("cb_r", [NL, 128, KE], F32, kind="ExternalInput").ap()
    dtb_r = nc.dram_tensor("dtb_r", [NL, 128, KE], F32, kind="ExternalInput").ap()
    dsk_r = nc.dram_tensor("dsk_r", [NL, 128, KE], F32, kind="ExternalInput").ap()
    alog_r = nc.dram_tensor("alog_r", [NL, 128, KE * S], F32, kind="ExternalInput").ap()
    sel_all = nc.dram_tensor("sel_all", [S, S, 128], BF16, kind="ExternalInput").ap()
    wg = nc.dram_tensor("wg", [D, HID], I32, kind="ExternalInput").ap()
    wu = nc.dram_tensor("wu", [D, HID], I32, kind="ExternalInput").ap()
    wd = nc.dram_tensor("wd", [HID, D], I32, kind="ExternalInput").ap()

    sc_in = nc.dram_tensor("sc_in", [NL, 2 * KE, 128, KD, 128], BF16,
                           kind="Internal").ap()
    sc_out = nc.dram_tensor("sc_out", [NL, KD, 128, KE, 128], BF16,
                            kind="Internal").ap()
    sc_gu = nc.dram_tensor("sc_gu", [KH, 128, 2 * KD, 128], BF16,
                           kind="Internal").ap()
    sc_d = nc.dram_tensor("sc_d", [KD, 128, KH, 128], BF16, kind="Internal").ap()

    es = contextlib.ExitStack()
    pr = Prog()
    with es:
        def T(n, shp, dt=F32):
            return Buf(es.enter_context(nc.sbuf_tensor(n, shp, dt)), n)
        def PT(n, shp):
            return Buf(es.enter_context(nc.psum_tensor(n, shp, F32)), n)

        ones_bf = T("ones_bf", [1, 128], BF16)
        ones_col = T("ones_col", [128, 1], F32)
        ones_row = T("ones_row", [1, 128], F32)
        epsrow = T("epsrow", [1, 1], F32)
        pr.op("vector", lambda h: h.memset(ones_bf.ap, 1.0), writes=[ones_bf])
        pr.op("vector", lambda h: h.memset(ones_col.ap, 1.0), writes=[ones_col])
        pr.op("vector", lambda h: h.memset(ones_row.ap, 1.0), writes=[ones_row])
        pr.op("vector", lambda h: h.memset(epsrow.ap, EPS), writes=[epsrow])

        cw = [T(f"cw{l}", [128, KE * 4]) for l in range(NL)]
        cb = [T(f"cb{l}", [128, KE]) for l in range(NL)]
        dtb = [T(f"dtb{l}", [128, KE]) for l in range(NL)]
        dsk = [T(f"dsk{l}", [128, KE]) for l in range(NL)]
        aneg = [T(f"aneg{l}", [128, KE * S]) for l in range(NL)]
        wxp_b = [T(f"wxp{l}", [128, KE * 192], BF16) for l in range(NL)]
        wdt_b = [T(f"wdtb{l}", [RK, E], BF16) for l in range(NL)]

        for l in range(NL):
            for (dst, src) in [(cw[l], cw_r[l]), (cb[l], cb_r[l]),
                               (dtb[l], dtb_r[l]), (dsk[l], dsk_r[l]),
                               (aneg[l], alog_r[l])]:
                pr.op("sync", lambda h, dst=dst, src=src: h.dma_start(
                    out=dst.ap, in_=src), writes=[dst], dma=True)
            pr.op("scalar", lambda h, l=l: h.activation(
                out=aneg[l].ap, in_=aneg[l].ap, func=AF.Exp),
                reads=[aneg[l]], writes=[aneg[l]])
            pr.op("vector", lambda h, l=l: h.tensor_scalar_mul(
                out=aneg[l].ap, in0=aneg[l].ap, scalar1=-1.0),
                reads=[aneg[l]], writes=[aneg[l]])

        stg_f = T("stg_f", [128, 256], F32)
        stg_i = Buf(es.enter_context(nc.sbuf_tensor("stg_i", [128, 256], I32)), "stg_i")
        stg_b = T("stg_b", [128, 256], BF16)
        scb = {"sc_in": Buf(None, "sc_in"), "sc_out": Buf(None, "sc_out"),
               "sc_gu": Buf(None, "sc_gu"), "sc_d": Buf(None, "sc_d")}

        for l in range(NL):
            for c in range(KE):
                pr.op("sync", lambda h, l=l, c=c: h.dma_start(
                    out=stg_f.h[:, 0:192], in_=w_xp[l, 128 * c:128 * (c + 1), :]),
                    writes=[stg_f], dma=True)
                pr.op("gpsimd", lambda h, l=l, c=c: h.tensor_copy(
                    out=wxp_b[l].h[:, 192 * c:192 * (c + 1)], in_=stg_f.h[:, 0:192]),
                    reads=[stg_f], writes=[wxp_b[l]])
            for g in range(8):
                pr.op("sync", lambda h, l=l, g=g: h.dma_start(
                    out=stg_f.h[0:RK, :], in_=w_dt[l, :, 256 * g:256 * (g + 1)]),
                    writes=[stg_f], dma=True)
                pr.op("gpsimd", lambda h, l=l, g=g: h.tensor_copy(
                    out=wdt_b[l].h[:, 256 * g:256 * (g + 1)], in_=stg_f.h[0:RK, :]),
                    reads=[stg_f], writes=[wdt_b[l]])

        def run_prepass(get_rows, get_dst, n_k, n_m, is_int):
            GM = 2
            stg_in = stg_i if is_int else stg_f
            for c in range(n_k):
                rows = get_rows(c)
                for g0 in range(0, n_m, GM):
                    gm = min(GM, n_m - g0)
                    pr.op("sync", lambda h, rows=rows, g0=g0, gm=gm, stg_in=stg_in:
                          h.dma_start(out=stg_in.h[:, 0:gm * 128],
                                      in_=rows[:, 128 * g0:128 * (g0 + gm)]),
                          writes=[stg_in], dma=True)
                    pr.op("gpsimd", lambda h, gm=gm, stg_in=stg_in: h.tensor_copy(
                        out=stg_b.h[:, 0:gm * 128], in_=stg_in.h[:, 0:gm * 128]),
                        reads=[stg_in], writes=[stg_b])
                    dst = get_dst(c, g0, gm)
                    pr.op("gpsimd", lambda h, dst=dst, gm=gm: h.dma_start(
                        out=dst.rearrange("a p b -> p a b"),
                        in_=stg_b.h[:, 0:gm * 128]
                        .rearrange("p (a b) -> p a b", a=gm)),
                        reads=[stg_b], writes=[scb_cur[0]], dma=True)

        for l in range(NL):
            scb_cur = [scb["sc_in"]]
            run_prepass(lambda c, l=l: w_in[l, 128 * c:128 * (c + 1), :],
                        lambda c, g0, gm, l=l: sc_in[l, g0:g0 + gm, :, c, :],
                        KD, 2 * KE, False)
            scb_cur = [scb["sc_out"]]
            run_prepass(lambda c, l=l: w_out[l, 128 * c:128 * (c + 1), :],
                        lambda c, g0, gm, l=l: sc_out[l, g0:g0 + gm, :, c, :],
                        KE, KD, False)
        scb_cur = [scb["sc_gu"]]
        run_prepass(lambda c: wg[128 * c:128 * (c + 1), :],
                    lambda c, g0, gm: sc_gu[g0:g0 + gm, :, c, :], KD, KH, True)
        run_prepass(lambda c: wu[128 * c:128 * (c + 1), :],
                    lambda c, g0, gm: sc_gu[g0:g0 + gm, :, KD + c, :], KD, KH, True)
        scb_cur = [scb["sc_d"]]
        run_prepass(lambda c: wd[128 * c:128 * (c + 1), :],
                    lambda c, g0, gm: sc_d[g0:g0 + gm, :, c, :], KH, KD, True)

        xf = [T(f"xf{k}", [128, TC]) for k in range(KD)]
        xb = [T(f"xb{k}", [128, TC], BF16) for k in range(KD)]
        u_ext = [T(f"uext{i}", [128, TC + 3], BF16) for i in range(4)]
        u_bf = [T(f"ubf{m}", [128, TC], BF16) for m in range(KE)]
        dt_bf = [T(f"dtbf{m}", [128, TC], BF16) for m in range(KE)]
        du_bf = [T(f"dubf{m}", [128, TC], BF16) for m in range(KE)]
        conv_acc = [T(f"cacc{i}", [128, TC]) for i in range(2)]
        sp1 = [T(f"sp1_{i}", [128, TC], BF16) for i in range(2)]
        sp2 = [T(f"sp2_{i}", [128, TC], BF16) for i in range(2)]
        dtr_bf = T("dtr_bf", [RK, TC], BF16)
        bmat_bf = T("bmat_bf", [S, TC], BF16)
        cmat_bf = T("cmat_bf", [S, TC], BF16)
        da_t = [T(f"da{i}", [128, 2 * TC]) for i in range(4)]
        b_t = [T(f"b{i}", [128, 2 * TC], BF16) for i in range(4)]
        h_t = [T(f"h{i}", [128, 2 * TC], BF16) for i in range(4)]
        pd_t = [T(f"pd{i}", [128, TC], BF16) for i in range(4)]
        ppd_t = [T(f"ppd{i}", [128, TC], BF16) for i in range(2)]
        bbs4 = [T(f"bbs{i}", [128, TC], BF16) for i in range(4)]
        ccs4 = [T(f"ccs{i}", [128, TC], BF16) for i in range(4)]
        yac = [T(f"yac{m}", [128, TC]) for m in range(8)]
        zs_t = [T(f"zs{i}", [128, TC]) for i in range(2)]
        sq_t = [T("sq_0", [128, TC])] * 2
        row_a = T("row_a", [1, TC])
        row_b = T("row_b", [1, TC])
        row_c = T("row_c", [1, TC])

        cvtail = [[T(f"cvt{l}_{m}", [128, 3], BF16) for m in range(KE)]
                  for l in range(NL)]
        hcar = [T(f"hcar{l}", [128, KE * S], BF16) for l in range(NL)]
        for l in range(NL):
            for m in range(KE):
                pr.op("vector", lambda h, l=l, m=m: h.memset(cvtail[l][m].ap, 0.0),
                      writes=[cvtail[l][m]])
            pr.op("vector", lambda h, l=l: h.memset(hcar[l].ap, 0.0),
                  writes=[hcar[l]])
        hcol = [[[Buf(hcar[l].h, f"hc{l}_{m}_{s}") for s in range(S)]
                 for m in range(KE)] for l in range(NL)]
        for l in range(NL):
            for m in range(KE):
                for s in range(S):
                    hcol[l][m][s].w = hcar[l].w

        wpool = [T(f"wpool{i}", [128, KE * 128], BF16) for i in range(2)]

        sel = [T(f"sel{i}", [S, 128], BF16) for i in range(2)]
        selab = Buf(None, "sel_all")

        pm = [PT(f"pm{i}", [128, TC]) for i in range(2)]
        pbc = [PT(f"pbc{i}", [128, TC]) for i in range(2)]
        pcc = [PT(f"pcc{i}", [128, TC]) for i in range(2)]
        pool6 = pm + pbc + pcc
        ps1 = PT("ps1", [1, TC])
        ps2 = PT("ps2", [1, TC])

        def mm_acc(dst_ps, out_ap_fn, pairs):
            n = len(pairs)
            for i, (lap, rhs, lbs) in enumerate(pairs):
                pr.op("tensor", lambda h, lap=lap, rhs=rhs, i=i:
                      h.matmul(out=out_ap_fn(), lhsT=lap(), rhs=rhs.ap,
                               start=(i == 0), stop=(i == n - 1)),
                      reads=[rhs] + lbs, writes=[dst_ps])

        for tc in range(NT):
            t0 = tc * TC
            for k in range(KD):
                pr.op("sync", lambda h, k=k, t0=t0: h.dma_start(
                    out=xf[k].ap, in_=xT[128 * k:128 * (k + 1), t0:t0 + TC]),
                    writes=[xf[k]], dma=True)
                pr.op("scalar", lambda h, k=k: h.activation(
                    out=xb[k].ap, in_=xf[k].ap, func=AF.Copy),
                    reads=[xf[k]], writes=[xb[k]])

            for l in range(NL):
                xin_bf = xb
                def inproj_front(m):
                    wt = wpool[m % 2]
                    pr.op("sync", lambda h, l=l, m=m, wt=wt: h.dma_start(
                        out=wt.h[:, 0:KD * 128],
                        in_=sc_in[l, m].rearrange("p a b -> p (a b)")),
                        reads=[scb["sc_in"]], writes=[wt], dma=True)
                    ps = pool6[m % 6]
                    mm_acc(ps, lambda ps=ps: ps.ap,
                           [((lambda wt=wt, c=c: wt.h[:, 128 * c:128 * (c + 1)]),
                             xin_bf[c], [wt]) for c in range(KD)])
                    ue = u_ext[m % 4]
                    pr.op("scalar", lambda h, ue=ue, ps=ps: h.activation(
                        out=ue.h[:, 3:TC + 3], in_=ps.ap, func=AF.Copy),
                        reads=[ps], writes=[ue])
                def inproj_back(m):
                    ue = u_ext[m % 4]
                    pr.op("vector", lambda h, ue=ue, l=l, m=m: h.tensor_copy(
                        out=ue.h[:, 0:3], in_=cvtail[l][m].ap),
                        reads=[cvtail[l][m]], writes=[ue])
                    pr.op("vector", lambda h, ue=ue, l=l, m=m: h.tensor_copy(
                        out=cvtail[l][m].ap, in_=ue.h[:, TC:TC + 3]),
                        reads=[ue], writes=[cvtail[l][m]])
                    ca = conv_acc[m % 2]
                    pr.op("vector", lambda h, ca=ca, ue=ue, l=l, m=m:
                          h.tensor_scalar_mul(
                              out=ca.ap, in0=ue.h[:, 0:TC],
                              scalar1=cw[l].h[:, 4 * m:4 * m + 1]),
                          reads=[ue, cw[l]], writes=[ca])
                    for k in range(1, 4):
                        pr.op("vector", lambda h, ca=ca, ue=ue, l=l, m=m, k=k:
                              h.scalar_tensor_tensor(
                                  out=ca.ap, in0=ue.h[:, k:k + TC],
                                  scalar=cw[l].h[:, 4 * m + k:4 * m + k + 1],
                                  in1=ca.ap, op0=ALU.mult, op1=ALU.add),
                              reads=[ue, cw[l], ca], writes=[ca])
                    pr.op("scalar", lambda h, ca=ca, l=l, m=m: h.activation(
                        out=u_bf[m].ap, in_=ca.ap, func=AF.Silu,
                        bias=cb[l].h[:, m:m + 1]),
                        reads=[ca, cb[l]], writes=[u_bf[m]])
                for m in range(KE + 2):
                    if m < KE:
                        inproj_front(m)
                    if m >= 2:
                        inproj_back(m - 2)

                for gi, (dst, off) in enumerate([(dtr_bf, 0), (bmat_bf, 64),
                                                 (cmat_bf, 128)]):
                    ps = pool6[gi % 6]
                    mm_acc(ps, lambda ps=ps: ps.h[0:64, :],
                           [((lambda c=c, l=l, off=off:
                              wxp_b[l].h[:, 192 * c + off:192 * c + off + 64]),
                             u_bf[c], [wxp_b[l]]) for c in range(KE)])
                    pr.op("scalar", lambda h, ps=ps, dst=dst: h.activation(
                        out=dst.ap, in_=ps.h[0:64, :], func=AF.Copy),
                        reads=[ps], writes=[dst])

                def dt_front(m):
                    ps = pool6[m % 6]
                    pr.op("tensor", lambda h, ps=ps, l=l, m=m: h.matmul(
                        out=ps.ap, lhsT=wdt_b[l].h[:, 128 * m:128 * (m + 1)],
                        rhs=dtr_bf.ap, start=True, stop=True),
                        reads=[wdt_b[l], dtr_bf], writes=[ps])
                    s1 = sp1[m % 2]; s2 = sp2[m % 2]
                    pr.op("scalar", lambda h, ps=ps, s1=s1, l=l, m=m: h.activation(
                        out=s1.ap, in_=ps.ap, func=AF.Abs, bias=dtb[l].h[:, m:m + 1]),
                        reads=[ps, dtb[l]], writes=[s1])
                    pr.op("scalar", lambda h, s1=s1: h.activation(
                        out=s1.ap, in_=s1.ap, func=AF.Exp, scale=-1.0),
                        reads=[s1], writes=[s1])
                    pr.op("scalar", lambda h, s1=s1: h.activation(
                        out=s1.ap, in_=s1.ap, func=AF.Ln, bias=1.0),
                        reads=[s1], writes=[s1])
                    pr.op("scalar", lambda h, ps=ps, s2=s2, l=l, m=m: h.activation(
                        out=s2.ap, in_=ps.ap, func=AF.Relu, bias=dtb[l].h[:, m:m + 1]),
                        reads=[ps, dtb[l]], writes=[s2])
                def dt_back(m):
                    s1 = sp1[m % 2]; s2 = sp2[m % 2]
                    pr.op("vector", lambda h, s1=s1, s2=s2, m=m: h.tensor_add(
                        out=dt_bf[m].ap, in0=s1.ap, in1=s2.ap),
                        reads=[s1, s2], writes=[dt_bf[m]])
                    pr.op("vector", lambda h, m=m: h.tensor_mul(
                        out=du_bf[m].ap, in0=dt_bf[m].ap, in1=u_bf[m].ap),
                        reads=[dt_bf[m], u_bf[m]], writes=[du_bf[m]])
                for m in range(KE + 2):
                    if m >= 2:
                        dt_back(m - 2)
                    if m < KE:
                        dt_front(m)

                for half in range(2):
                    ms = list(range(8 * half, 8 * half + 8))
                    for mi, m in enumerate(ms):
                        pr.op("vector", lambda h, mi=mi, m=m, l=l: h.tensor_scalar_mul(
                            out=yac[mi].ap, in0=u_bf[m].ap,
                            scalar1=dsk[l].h[:, m:m + 1]),
                            reads=[u_bf[m], dsk[l]], writes=[yac[mi]])
                    items = [(s2, mi) for s2 in range(0, S, 2) for mi in range(8)]
                    NI = len(items)
                    bix = lambda s2: 2 * ((s2 // 2) % 2)

                    def emit_bcast(s2):
                        bi = bix(s2)
                        for ds in range(2):
                            s = s2 + ds
                            sl = sel[ds]
                            pr.op("sync", lambda h, sl=sl, s=s: h.dma_start(
                                out=sl.ap, in_=sel_all[s]),
                                reads=[selab], writes=[sl], dma=True)
                            bb = pbc[ds]; cc = pcc[ds]
                            pr.op("tensor", lambda h, bb=bb, sl=sl: h.matmul(
                                out=bb.ap, lhsT=sl.ap, rhs=bmat_bf.ap,
                                start=True, stop=True),
                                reads=[sl, bmat_bf], writes=[bb])
                            pr.op("tensor", lambda h, cc=cc, sl=sl: h.matmul(
                                out=cc.ap, lhsT=sl.ap, rhs=cmat_bf.ap,
                                start=True, stop=True),
                                reads=[sl, cmat_bf], writes=[cc])
                            pr.op("scalar", lambda h, t=bbs4[bi + ds], bb=bb:
                                  h.activation(out=t.ap, in_=bb.ap, func=AF.Copy),
                                  reads=[bb], writes=[bbs4[bi + ds]])
                            pr.op("scalar", lambda h, t=ccs4[bi + ds], cc=cc:
                                  h.activation(out=t.ap, in_=cc.ap, func=AF.Copy),
                                  reads=[cc], writes=[ccs4[bi + ds]])

                    emit_bcast(0)
                    G3 = 2
                    NSG = NI // G3
                    def sg_items(j):
                        return [(k, items[k]) for k in
                                range(j * G3, min(NI, (j + 1) * G3)) if k < NI]
                    for j in range(NSG + 2):
                        for (k, (s2, mi)) in sg_items(j):
                            m = ms[mi]
                            if mi == 0 and s2 + 2 < S:
                                emit_bcast(s2 + 2)
                            for ds in range(2):
                                s = s2 + ds
                                da = da_t[k % 4]
                                pr.op("scalar", lambda h, da=da, ds=ds, l=l, m=m, s=s:
                                      h.activation(
                                          out=da.h[:, ds * TC:(ds + 1) * TC],
                                          in_=dt_bf[m].ap, func=AF.Exp,
                                          scale=aneg[l].h[:, m * S + s:m * S + s + 1]),
                                      reads=[dt_bf[m], aneg[l]], writes=[da])
                        for (k, (s2, mi)) in sg_items(j):
                            m = ms[mi]
                            for ds in range(2):
                                bt = b_t[k % 4]
                                bbuf = bbs4[bix(s2) + ds]
                                pr.op("vector", lambda h, bt=bt, ds=ds, m=m, bbuf=bbuf:
                                      h.tensor_mul(
                                          out=bt.h[:, ds * TC:(ds + 1) * TC],
                                          in0=du_bf[m].ap, in1=bbuf.ap),
                                      reads=[du_bf[m], bbuf], writes=[bt])
                        if 2 <= j:
                            for (k, (s2, mi)) in sg_items(j - 2):
                                for ds in range(2):
                                    ht = h_t[k % 4]
                                    pd = pd_t[(k * 2 + ds) % 4]
                                    cbuf = ccs4[bix(s2) + ds]
                                    pr.op("vector",
                                          lambda h, pd=pd, ht=ht, ds=ds, cbuf=cbuf:
                                          h.tensor_mul(
                                              out=pd.ap,
                                              in0=ht.h[:, ds * TC:(ds + 1) * TC],
                                              in1=cbuf.ap),
                                          reads=[ht, cbuf], writes=[pd])
                            for (k, (s2, mi)) in sg_items(j - 2):
                                pp = ppd_t[k % 2]
                                pa = pd_t[(k * 2) % 4]; pb = pd_t[(k * 2 + 1) % 4]
                                pr.op("vector", lambda h, pp=pp, pa=pa, pb=pb:
                                      h.tensor_add(out=pp.ap, in0=pa.ap, in1=pb.ap),
                                      reads=[pa, pb], writes=[pp])
                            for (k, (s2, mi)) in sg_items(j - 2):
                                pp = ppd_t[k % 2]
                                ya = yac[mi]
                                pr.op("vector", lambda h, ya=ya, pp=pp: h.tensor_add(
                                    out=ya.ap, in0=ya.ap, in1=pp.ap),
                                    reads=[pp, ya], writes=[ya])
                            for (k, (s2, mi)) in sg_items(j - 2):
                                m = ms[mi]
                                ht = h_t[k % 4]
                                hc0 = hcol[l][m][s2]; hc1 = hcol[l][m][s2 + 1]
                                col = m * S + s2
                                pr.op("scalar", lambda h, ht=ht, hc0=hc0, col=col:
                                      h.activation(
                                          out=hc0.h[:, col:col + 2],
                                          in_=ht.h[:, TC - 1:2 * TC:TC],
                                          func=AF.Copy),
                                      reads=[ht], writes=[hc0, hc1])
                        if 1 <= j:
                            for (k, (s2, mi)) in sg_items(j - 1):
                                m = ms[mi]
                                da = da_t[k % 4]; bt = b_t[k % 4]
                                hc1 = hcol[l][m][s2 + 1]
                                col1 = m * S + s2 + 1
                                pr.op("vector", lambda h, bt=bt, da=da, hc1=hc1,
                                      col1=col1: h.scalar_tensor_tensor(
                                          out=bt.h[:, TC:TC + 1],
                                          in0=hc1.h[:, col1:col1 + 1],
                                          scalar=da.h[:, TC:TC + 1],
                                          in1=bt.h[:, TC:TC + 1],
                                          op0=ALU.mult, op1=ALU.add),
                                      reads=[hc1, da, bt], writes=[bt])
                                pr.op("vector", lambda h, da=da: h.memset(
                                    da.h[:, TC:TC + 1], 0.0),
                                    reads=[], writes=[da])
                            for (k, (s2, mi)) in sg_items(j - 1):
                                m = ms[mi]
                                da = da_t[k % 4]; bt = b_t[k % 4]; ht = h_t[k % 4]
                                hc0 = hcol[l][m][s2]
                                col0 = m * S + s2
                                pr.op("vector",
                                      lambda h, ht=ht, da=da, bt=bt, hc0=hc0,
                                      col0=col0:
                                      h.tensor_tensor_scan(
                                          out=ht.ap, data0=da.ap, data1=bt.ap,
                                          initial=hc0.h[:, col0:col0 + 1],
                                          op0=ALU.mult, op1=ALU.add),
                                      reads=[da, bt, hc0], writes=[ht])
                    def z_front(mi, m):
                        wt = wpool[m % 2]
                        pr.op("sync", lambda h, l=l, m=m, wt=wt: h.dma_start(
                            out=wt.h[:, 0:KD * 128],
                            in_=sc_in[l, KE + m].rearrange("p a b -> p (a b)")),
                            reads=[scb["sc_in"]], writes=[wt], dma=True)
                        ps = pool6[m % 6]
                        mm_acc(ps, lambda ps=ps: ps.ap,
                               [((lambda wt=wt, c=c: wt.h[:, 128 * c:128 * (c + 1)]),
                                 xin_bf[c], [wt]) for c in range(KD)])
                        zs = zs_t[mi % 2]
                        pr.op("scalar", lambda h, zs=zs, ps=ps: h.activation(
                            out=zs.ap, in_=ps.ap, func=AF.Silu),
                            reads=[ps], writes=[zs])
                    def z_back(mi, m):
                        ya = yac[mi]
                        zs = zs_t[mi % 2]
                        pr.op("vector", lambda h, ya=ya, zs=zs, m=m: h.tensor_mul(
                            out=du_bf[m].ap, in0=ya.ap, in1=zs.ap),
                            reads=[ya, zs], writes=[du_bf[m]])
                    for mi in range(10):
                        if mi >= 2:
                            z_back(mi - 2, ms[mi - 2])
                        if mi < 8:
                            z_front(mi, ms[mi])

                def op_front(k):
                    wt = wpool[k % 2]
                    pr.op("sync", lambda h, l=l, k=k, wt=wt: h.dma_start(
                        out=wt.h[:, 0:KE * 128],
                        in_=sc_out[l, k].rearrange("p a b -> p (a b)")),
                        reads=[scb["sc_out"]], writes=[wt], dma=True)
                    ps = pool6[k % 6]
                    mm_acc(ps, lambda ps=ps: ps.ap,
                           [((lambda wt=wt, c=c: wt.h[:, 128 * c:128 * (c + 1)]),
                             du_bf[c], [wt]) for c in range(KE)])
                def op_back(k):
                    ps = pool6[k % 6]
                    if l == 0:
                        pr.op("scalar", lambda h, ps=ps, k=k: h.activation(
                            out=xb[k].ap, in_=ps.ap, func=AF.Copy),
                            reads=[ps], writes=[xb[k]])
                    else:
                        pr.op("vector", lambda h, ps=ps, k=k: h.tensor_add(
                            out=xf[k].ap, in0=ps.ap, in1=xf[k].ap),
                            reads=[ps, xf[k]], writes=[xf[k]])
                for k in range(KD + 2):
                    if k < KD:
                        op_front(k)
                    if k >= 2:
                        op_back(k - 2)
            def layernorm(src_f, dst_f, dst_b):
                for k in range(KD):
                    pr.op("tensor", lambda h, k=k: h.matmul(
                        out=ps1.ap, lhsT=ones_col.ap, rhs=src_f[k].ap,
                        start=(k == 0), stop=(k == KD - 1)),
                        reads=[ones_col, src_f[k]], writes=[ps1])
                for k in range(KD):
                    sq = sq_t[k % 2]
                    pr.op("scalar", lambda h, k=k, sq=sq: h.activation(
                        out=sq.ap, in_=src_f[k].ap, func=AF.Square),
                        reads=[src_f[k]], writes=[sq])
                    pr.op("tensor", lambda h, k=k, sq=sq: h.matmul(
                        out=ps2.ap, lhsT=ones_col.ap, rhs=sq.ap,
                        start=(k == 0), stop=(k == KD - 1)),
                        reads=[ones_col, sq], writes=[ps2])
                pr.op("vector", lambda h: h.tensor_scalar_mul(
                    out=row_a.ap, in0=ps1.ap, scalar1=1.0 / D),
                    reads=[ps1], writes=[row_a])
                pr.op("vector", lambda h: h.tensor_mul(
                    out=row_b.ap, in0=row_a.ap, in1=row_a.ap),
                    reads=[row_a], writes=[row_b])
                pr.op("vector", lambda h: h.scalar_tensor_tensor(
                    out=row_b.ap, in0=ps2.ap, scalar=1.0 / D, in1=row_b.ap,
                    op0=ALU.mult, op1=ALU.subtract),
                    reads=[ps2, row_b], writes=[row_b])
                pr.op("scalar", lambda h: h.activation(
                    out=row_b.ap, in_=row_b.ap, func=AF.Ln, bias=epsrow.ap),
                    reads=[row_b, epsrow], writes=[row_b])
                pr.op("scalar", lambda h: h.activation(
                    out=row_b.ap, in_=row_b.ap, func=AF.Exp, scale=-0.5),
                    reads=[row_b], writes=[row_b])
                pr.op("vector", lambda h: h.tensor_mul(
                    out=row_c.ap, in0=row_a.ap, in1=row_b.ap),
                    reads=[row_a, row_b], writes=[row_c])
                pr.op("tensor", lambda h: h.matmul(
                    out=pbc[0].ap, lhsT=ones_row.ap, rhs=row_b.ap,
                    start=True, stop=True),
                    reads=[ones_row, row_b], writes=[pbc[0]])
                pr.op("tensor", lambda h: h.matmul(
                    out=pcc[0].ap, lhsT=ones_row.ap, rhs=row_c.ap,
                    start=True, stop=True),
                    reads=[ones_row, row_c], writes=[pcc[0]])
                for k in range(KD):
                    pr.op("vector", lambda h, k=k: h.tensor_mul(
                        out=dst_f[k].ap, in0=src_f[k].ap, in1=pbc[0].ap),
                        reads=[src_f[k], pbc[0]], writes=[dst_f[k]])
                    pr.op("vector", lambda h, k=k: h.tensor_sub(
                        out=dst_f[k].ap, in0=dst_f[k].ap, in1=pcc[0].ap),
                        reads=[dst_f[k], pcc[0]], writes=[dst_f[k]])
                    if dst_b is not None:
                        pr.op("scalar", lambda h, k=k: h.activation(
                            out=dst_b[k].ap, in_=dst_f[k].ap, func=AF.Copy),
                            reads=[dst_f[k]], writes=[dst_b[k]])

            layernorm(xf, xf, xb)

            hbt = u_bf + dt_bf[:KH - KE]
            def ffn_front(m):
                wt = wpool[m % 2]
                pr.op("sync", lambda h, m=m, wt=wt: h.dma_start(
                    out=wt.h[:, 0:2 * KD * 128],
                    in_=sc_gu[m].rearrange("p a b -> p (a b)")),
                    reads=[scb["sc_gu"]], writes=[wt], dma=True)
                psg = pool6[(2 * m) % 6]; psu = pool6[(2 * m + 1) % 6]
                mm_acc(psg, lambda psg=psg: psg.ap,
                       [((lambda wt=wt, c=c: wt.h[:, 128 * c:128 * (c + 1)]),
                         xb[c], [wt]) for c in range(KD)])
                mm_acc(psu, lambda psu=psu: psu.ap,
                       [((lambda wt=wt, c=c: wt.h[:, 128 * (KD + c):128 * (KD + c + 1)]),
                         xb[c], [wt]) for c in range(KD)])
                gs = zs_t[m % 2]
                pr.op("scalar", lambda h, gs=gs, psg=psg: h.activation(
                    out=gs.ap, in_=psg.ap, func=AF.Silu, scale=1.0 / 127.0),
                    reads=[psg], writes=[gs])
            def ffn_back(m):
                psu = pool6[(2 * m + 1) % 6]
                gs = zs_t[m % 2]
                pr.op("vector", lambda h, m=m, gs=gs, psu=psu: h.scalar_tensor_tensor(
                    out=hbt[m].ap, in0=psu.ap, scalar=1.0 / 127.0, in1=gs.ap,
                    op0=ALU.mult, op1=ALU.mult),
                    reads=[psu, gs], writes=[hbt[m]])
            for m in range(KH + 1):
                if m >= 1:
                    ffn_back(m - 1)
                if m < KH:
                    ffn_front(m)
            for k in range(KD):
                wta, wtb = wpool[0], wpool[1]
                pr.op("sync", lambda h, k=k, wta=wta: h.dma_start(
                    out=wta.h[:, 0:12 * 128],
                    in_=sc_d[k, :, 0:12, :].rearrange("p a b -> p (a b)")),
                    reads=[scb["sc_d"]], writes=[wta], dma=True)
                pr.op("sync", lambda h, k=k, wtb=wtb: h.dma_start(
                    out=wtb.h[:, 0:12 * 128],
                    in_=sc_d[k, :, 12:24, :].rearrange("p a b -> p (a b)")),
                    reads=[scb["sc_d"]], writes=[wtb], dma=True)
                ps = pool6[k % 6]
                mm_acc(ps, lambda ps=ps: ps.ap,
                       [((lambda wt=(wta if c < 12 else wtb), c=c:
                          wt.h[:, 128 * (c % 12):128 * (c % 12 + 1)]),
                         hbt[c], [wta if c < 12 else wtb]) for c in range(KH)])
                pr.op("vector", lambda h, ps=ps, k=k: h.scalar_tensor_tensor(
                    out=xf[k].ap, in0=ps.ap, scalar=1.0 / 127.0, in1=xf[k].ap,
                    op0=ALU.mult, op1=ALU.add),
                    reads=[ps, xf[k]], writes=[xf[k]])
            layernorm(xf, xf, None)

            for k in range(KD):
                pr.op("gpsimd", lambda h, k=k, t0=t0: h.dma_start(
                    out=outT[128 * k:128 * (k + 1), t0:t0 + TC], in_=xf[k].ap),
                    reads=[xf[k]], dma=True)

        pr.wait("gpsimd", pr.barrier_tok("dma_gpsimd"))
        pr.emit(nc, es)
    return nc



def _prep_inputs(L0_attn_x, L0_attn_in_proj_w, L0_attn_conv_w, L0_attn_conv_b,
                 L0_attn_x_proj_w, L0_attn_dt_proj_w, L0_attn_dt_proj_b,
                 L0_attn_A_log, L0_attn_D_skip, L0_attn_out_proj_w,
                 L0_ffn_w_gate, L0_ffn_w_up, L0_ffn_w_down):
    C = lambda a: np.ascontiguousarray(a)
    A = lambda a: np.asarray(a)
    shared = {
        "w_in": C(np.transpose(A(L0_attn_in_proj_w), (0, 2, 1))),
        "w_xp": C(np.transpose(A(L0_attn_x_proj_w), (0, 2, 1))),
        "w_dt": C(np.transpose(A(L0_attn_dt_proj_w), (0, 2, 1))),
        "w_out": C(np.transpose(A(L0_attn_out_proj_w), (0, 2, 1))),
        "cw_r": C(A(L0_attn_conv_w).reshape(NL, KE, 128, 4)
                  .transpose(0, 2, 1, 3).reshape(NL, 128, KE * 4)),
        "cb_r": C(A(L0_attn_conv_b).reshape(NL, KE, 128).transpose(0, 2, 1)),
        "dtb_r": C(A(L0_attn_dt_proj_b).reshape(NL, KE, 128).transpose(0, 2, 1)),
        "dsk_r": C(A(L0_attn_D_skip).reshape(NL, KE, 128).transpose(0, 2, 1)),
        "alog_r": C(A(L0_attn_A_log).reshape(NL, KE, 128, S)
                    .transpose(0, 2, 1, 3).reshape(NL, 128, KE * S)),
        "sel_all": np.repeat(np.eye(S, dtype=np.float32)[:, :, None]
                             .astype(ml_dtypes.bfloat16), 128, axis=2),
        "wg": C(A(L0_ffn_w_gate).T),
        "wu": C(A(L0_ffn_w_up).T),
        "wd": C(A(L0_ffn_w_down).T),
    }
    x = A(L0_attn_x)
    per_core = [dict(shared, xT=C(x[i].T)) for i in range(x.shape[0])]
    return per_core


def kernel(**inputs):
    per_core = _prep_inputs(**inputs)
    nc = build_core()
    res = run_bass_kernel_spmd(nc, per_core, core_ids=list(range(8)))
    outs = [np.ascontiguousarray(r["outT"].T) for r in res.results]
    return np.stack(outs, 0).astype(np.float32)


# version 21
# speedup vs baseline: 1.0831x; 1.0831x over previous
import contextlib
import numpy as np
import ml_dtypes
import concourse.bass as bass
import concourse.mybir as mybir
from concourse.bass_utils import run_bass_kernel_spmd

F32 = mybir.dt.float32
BF16 = mybir.dt.bfloat16
I32 = mybir.dt.int32
AF = mybir.ActivationFunctionType
ALU = mybir.AluOpType


class Tok:
    __slots__ = ("k", "v")
    def __init__(self, k, v):
        self.k = k
        self.v = v


class Buf:
    __slots__ = ("h", "w", "r", "name")
    def __init__(self, h, name=""):
        self.h = h
        self.name = name or getattr(h, "name", "")
        self.w = None
        self.r = []
    def __getitem__(self, idx):
        return self.h[idx]
    @property
    def ap(self):
        return self.h[:]


ENGINES = ("tensor", "vector", "scalar", "gpsimd", "sync")


class Prog:
    def __init__(self):
        self.items = {e: [] for e in ENGINES}
        self.counts = {}
        for e in ENGINES:
            self.counts[e] = 0
            self.counts["dma_" + e] = 0
        self.wm = {e: {} for e in ENGINES}
        self.pending = {"dma_" + e: [] for e in ENGINES}

    def _fence(self, key):
        plist = self.pending[key]
        if not plist:
            return
        total = self.counts[key]
        for t in plist:
            t.v = total
        plist.clear()
        eng = key[4:]
        self.items[eng].append(([(key, total)], None, None, 0))
        wm = self.wm[eng]
        if wm.get(key, 0) < total:
            wm[key] = total

    def op(self, eng, fn, reads=(), writes=(), dma=False):
        toks = []
        for b in reads:
            if b.w:
                toks.append(b.w)
        for b in writes:
            if b.w:
                toks.append(b.w)
            toks.extend(b.r)
        for t in toks:
            if t.v is None:
                self._fence(t.k)
        best = {}
        for t in toks:
            if t.v > best.get(t.k, -1):
                best[t.k] = t.v
        wm = self.wm[eng]
        waits = []
        for k in sorted(best):
            v = best[k]
            if wm.get(k, 0) < v:
                waits.append((k, v))
                wm[k] = v
        key = ("dma_" + eng) if dma else eng
        step = 16 if dma else 1
        self.counts[key] += step
        if dma:
            tok = Tok(key, None)
            self.pending[key].append(tok)
        else:
            tok = Tok(key, self.counts[key])
        self.items[eng].append((waits, fn, key, step))
        for b in writes:
            b.w = tok
            b.r = []
        for b in reads:
            b.r.append(tok)
        return tok

    def barrier_tok(self, key):
        if key.startswith("dma_"):
            self._fence(key)
        return Tok(key, self.counts[key])

    def wait(self, eng, tok):
        if tok.v is None:
            self._fence(tok.k)
        wm = self.wm[eng]
        if wm.get(tok.k, 0) >= tok.v:
            return
        wm[tok.k] = tok.v
        self.items[eng].append(([(tok.k, tok.v)], None, None, 0))

    def emit(self, nc, es):
        used = set(k for lst in self.items.values()
                   for (ws, _f, _k, _s) in lst for (k, _v) in ws)
        used |= set(k for k, c in self.counts.items() if c > 0)
        sems = {k: es.enter_context(nc.semaphore("sq_" + k)) for k in sorted(used)}
        block = es.enter_context(nc.Block())
        for e in ENGINES:
            lst = self.items[e]
            if not lst:
                continue
            def make(lst):
                def body(h):
                    for (waits, fn, key, step) in lst:
                        for (k, v) in waits:
                            h.wait_ge(sems[k], v)
                        if fn is not None:
                            fn(h).then_inc(sems[key], step)
                return body
            getattr(block, e)(make(lst))



D = 1024
E = 2048
S = 64
RK = 64
HID = 3072
NL = 2
KD = D // 128
KE = E // 128
KH = HID // 128
EPS = 1e-5


def build_core(L=4096, TC=512):
    NT = L // TC
    nc = bass.Bass("TRN2", target_bir_lowering=False)

    xT = nc.dram_tensor("xT", [D, L], F32, kind="ExternalInput").ap()
    outT = nc.dram_tensor("outT", [D, L], F32, kind="ExternalOutput").ap()
    w_in = nc.dram_tensor("w_in", [NL, D, 2 * E], F32, kind="ExternalInput").ap()
    w_xp = nc.dram_tensor("w_xp", [NL, E, 192], F32, kind="ExternalInput").ap()
    w_dt = nc.dram_tensor("w_dt", [NL, RK, E], F32, kind="ExternalInput").ap()
    w_out = nc.dram_tensor("w_out", [NL, E, D], F32, kind="ExternalInput").ap()
    cw_r = nc.dram_tensor("cw_r", [NL, 128, KE * 4], F32, kind="ExternalInput").ap()
    cb_r = nc.dram_tensor("cb_r", [NL, 128, KE], F32, kind="ExternalInput").ap()
    dtb_r = nc.dram_tensor("dtb_r", [NL, 128, KE], F32, kind="ExternalInput").ap()
    dsk_r = nc.dram_tensor("dsk_r", [NL, 128, KE], F32, kind="ExternalInput").ap()
    alog_r = nc.dram_tensor("alog_r", [NL, 128, KE * S], F32, kind="ExternalInput").ap()
    sel_all = nc.dram_tensor("sel_all", [S, S, 128], BF16, kind="ExternalInput").ap()
    wg = nc.dram_tensor("wg", [D, HID], I32, kind="ExternalInput").ap()
    wu = nc.dram_tensor("wu", [D, HID], I32, kind="ExternalInput").ap()
    wd = nc.dram_tensor("wd", [HID, D], I32, kind="ExternalInput").ap()

    sc_in = nc.dram_tensor("sc_in", [NL, 2 * KE, 128, KD, 128], BF16,
                           kind="Internal").ap()
    sc_out = nc.dram_tensor("sc_out", [NL, KD, 128, KE, 128], BF16,
                            kind="Internal").ap()
    sc_gu = nc.dram_tensor("sc_gu", [KH, 128, 2 * KD, 128], BF16,
                           kind="Internal").ap()
    sc_d = nc.dram_tensor("sc_d", [KD, 128, KH, 128], BF16, kind="Internal").ap()

    es = contextlib.ExitStack()
    pr = Prog()
    with es:
        def T(n, shp, dt=F32):
            return Buf(es.enter_context(nc.sbuf_tensor(n, shp, dt)), n)
        def PT(n, shp):
            return Buf(es.enter_context(nc.psum_tensor(n, shp, F32)), n)

        ones_bf = T("ones_bf", [1, 128], BF16)
        ones_col = T("ones_col", [128, 1], F32)
        ones_row = T("ones_row", [1, 128], F32)
        epsrow = T("epsrow", [1, 1], F32)
        pr.op("vector", lambda h: h.memset(ones_bf.ap, 1.0), writes=[ones_bf])
        pr.op("vector", lambda h: h.memset(ones_col.ap, 1.0), writes=[ones_col])
        pr.op("vector", lambda h: h.memset(ones_row.ap, 1.0), writes=[ones_row])
        pr.op("vector", lambda h: h.memset(epsrow.ap, EPS), writes=[epsrow])

        cw = [T(f"cw{l}", [128, KE * 4]) for l in range(NL)]
        cb = [T(f"cb{l}", [128, KE]) for l in range(NL)]
        dtb = [T(f"dtb{l}", [128, KE]) for l in range(NL)]
        dsk = [T(f"dsk{l}", [128, KE]) for l in range(NL)]
        aneg = [T(f"aneg{l}", [128, KE * S]) for l in range(NL)]
        wxp_b = [T(f"wxp{l}", [128, KE * 192], BF16) for l in range(NL)]
        wdt_b = [T(f"wdtb{l}", [RK, E], BF16) for l in range(NL)]

        for l in range(NL):
            for (dst, src) in [(cw[l], cw_r[l]), (cb[l], cb_r[l]),
                               (dtb[l], dtb_r[l]), (dsk[l], dsk_r[l]),
                               (aneg[l], alog_r[l])]:
                pr.op("sync", lambda h, dst=dst, src=src: h.dma_start(
                    out=dst.ap, in_=src), writes=[dst], dma=True)
            pr.op("scalar", lambda h, l=l: h.activation(
                out=aneg[l].ap, in_=aneg[l].ap, func=AF.Exp),
                reads=[aneg[l]], writes=[aneg[l]])
            pr.op("vector", lambda h, l=l: h.tensor_scalar_mul(
                out=aneg[l].ap, in0=aneg[l].ap, scalar1=-1.0),
                reads=[aneg[l]], writes=[aneg[l]])

        stg_f = T("stg_f", [128, 256], F32)
        stg_i = Buf(es.enter_context(nc.sbuf_tensor("stg_i", [128, 256], I32)), "stg_i")
        stg_b = T("stg_b", [128, 256], BF16)
        scb = {"sc_in": Buf(None, "sc_in"), "sc_out": Buf(None, "sc_out"),
               "sc_gu": Buf(None, "sc_gu"), "sc_d": Buf(None, "sc_d")}

        for l in range(NL):
            for c in range(KE):
                pr.op("sync", lambda h, l=l, c=c: h.dma_start(
                    out=stg_f.h[:, 0:192], in_=w_xp[l, 128 * c:128 * (c + 1), :]),
                    writes=[stg_f], dma=True)
                pr.op("gpsimd", lambda h, l=l, c=c: h.tensor_copy(
                    out=wxp_b[l].h[:, 192 * c:192 * (c + 1)], in_=stg_f.h[:, 0:192]),
                    reads=[stg_f], writes=[wxp_b[l]])
            for g in range(8):
                pr.op("sync", lambda h, l=l, g=g: h.dma_start(
                    out=stg_f.h[0:RK, :], in_=w_dt[l, :, 256 * g:256 * (g + 1)]),
                    writes=[stg_f], dma=True)
                pr.op("gpsimd", lambda h, l=l, g=g: h.tensor_copy(
                    out=wdt_b[l].h[:, 256 * g:256 * (g + 1)], in_=stg_f.h[0:RK, :]),
                    reads=[stg_f], writes=[wdt_b[l]])

        def run_prepass(get_rows, get_dst, n_k, n_m, is_int):
            GM = 2
            stg_in = stg_i if is_int else stg_f
            for c in range(n_k):
                rows = get_rows(c)
                for g0 in range(0, n_m, GM):
                    gm = min(GM, n_m - g0)
                    pr.op("sync", lambda h, rows=rows, g0=g0, gm=gm, stg_in=stg_in:
                          h.dma_start(out=stg_in.h[:, 0:gm * 128],
                                      in_=rows[:, 128 * g0:128 * (g0 + gm)]),
                          writes=[stg_in], dma=True)
                    pr.op("gpsimd", lambda h, gm=gm, stg_in=stg_in: h.tensor_copy(
                        out=stg_b.h[:, 0:gm * 128], in_=stg_in.h[:, 0:gm * 128]),
                        reads=[stg_in], writes=[stg_b])
                    dst = get_dst(c, g0, gm)
                    pr.op("gpsimd", lambda h, dst=dst, gm=gm: h.dma_start(
                        out=dst.rearrange("a p b -> p a b"),
                        in_=stg_b.h[:, 0:gm * 128]
                        .rearrange("p (a b) -> p a b", a=gm)),
                        reads=[stg_b], writes=[scb_cur[0]], dma=True)

        for l in range(NL):
            scb_cur = [scb["sc_in"]]
            run_prepass(lambda c, l=l: w_in[l, 128 * c:128 * (c + 1), :],
                        lambda c, g0, gm, l=l: sc_in[l, g0:g0 + gm, :, c, :],
                        KD, 2 * KE, False)
            scb_cur = [scb["sc_out"]]
            run_prepass(lambda c, l=l: w_out[l, 128 * c:128 * (c + 1), :],
                        lambda c, g0, gm, l=l: sc_out[l, g0:g0 + gm, :, c, :],
                        KE, KD, False)
        scb_cur = [scb["sc_gu"]]
        run_prepass(lambda c: wg[128 * c:128 * (c + 1), :],
                    lambda c, g0, gm: sc_gu[g0:g0 + gm, :, c, :], KD, KH, True)
        run_prepass(lambda c: wu[128 * c:128 * (c + 1), :],
                    lambda c, g0, gm: sc_gu[g0:g0 + gm, :, KD + c, :], KD, KH, True)
        scb_cur = [scb["sc_d"]]
        run_prepass(lambda c: wd[128 * c:128 * (c + 1), :],
                    lambda c, g0, gm: sc_d[g0:g0 + gm, :, c, :], KH, KD, True)

        xf = [T(f"xf{k}", [128, TC]) for k in range(KD)]
        xb = [T(f"xb{k}", [128, TC], BF16) for k in range(KD)]
        u_ext = [T(f"uext{i}", [128, TC + 3], BF16) for i in range(4)]
        u_bf = [T(f"ubf{m}", [128, TC], BF16) for m in range(KE)]
        dt_bf = [T(f"dtbf{m}", [128, TC], BF16) for m in range(KE)]
        du_bf = [T(f"dubf{m}", [128, TC], BF16) for m in range(KE)]
        conv_acc = [T(f"cacc{i}", [128, TC]) for i in range(2)]
        sp1 = [T(f"sp1_{i}", [128, TC], BF16) for i in range(2)]
        sp2 = [T(f"sp2_{i}", [128, TC], BF16) for i in range(2)]
        dtr_bf = T("dtr_bf", [RK, TC], BF16)
        bmat_bf = T("bmat_bf", [S, TC], BF16)
        cmat_bf = T("cmat_bf", [S, TC], BF16)
        da_t = [T(f"da{i}", [128, TC]) for i in range(8)]
        b_t = [T(f"b{i}", [128, TC], BF16) for i in range(8)]
        h_t = [T(f"h{i}", [128, TC], BF16) for i in range(8)]
        pd_t = [T(f"pd{i}", [128, TC], BF16) for i in range(4)]
        ppd_t = [T(f"ppd{i}", [128, TC], BF16) for i in range(2)]
        bbs4 = [T(f"bbs{i}", [128, TC], BF16) for i in range(4)]
        ccs4 = [T(f"ccs{i}", [128, TC], BF16) for i in range(4)]
        yac = [T(f"yac{m}", [128, TC]) for m in range(8)]
        zs_t = [T(f"zs{i}", [128, TC]) for i in range(2)]
        sq_t = [T("sq_0", [128, TC])] * 2
        row_a = T("row_a", [1, TC])
        row_b = T("row_b", [1, TC])
        row_c = T("row_c", [1, TC])

        cvtail = [[T(f"cvt{l}_{m}", [128, 3], BF16) for m in range(KE)]
                  for l in range(NL)]
        hcar = [T(f"hcar{l}", [128, KE * S], BF16) for l in range(NL)]
        for l in range(NL):
            for m in range(KE):
                pr.op("vector", lambda h, l=l, m=m: h.memset(cvtail[l][m].ap, 0.0),
                      writes=[cvtail[l][m]])
            pr.op("vector", lambda h, l=l: h.memset(hcar[l].ap, 0.0),
                  writes=[hcar[l]])
        hcol = [[[Buf(hcar[l].h, f"hc{l}_{m}_{s}") for s in range(S)]
                 for m in range(KE)] for l in range(NL)]
        for l in range(NL):
            for m in range(KE):
                for s in range(S):
                    hcol[l][m][s].w = hcar[l].w

        wpool = [T(f"wpool{i}", [128, KE * 128], BF16) for i in range(2)]

        sel = [T(f"sel{i}", [S, 128], BF16) for i in range(2)]
        selab = Buf(None, "sel_all")

        pm = [PT(f"pm{i}", [128, TC]) for i in range(2)]
        pbc = [PT(f"pbc{i}", [128, TC]) for i in range(2)]
        pcc = [PT(f"pcc{i}", [128, TC]) for i in range(2)]
        pool6 = pm + pbc + pcc
        ps1 = PT("ps1", [1, TC])
        ps2 = PT("ps2", [1, TC])

        def mm_acc(dst_ps, out_ap_fn, pairs):
            n = len(pairs)
            for i, (lap, rhs, lbs) in enumerate(pairs):
                pr.op("tensor", lambda h, lap=lap, rhs=rhs, i=i:
                      h.matmul(out=out_ap_fn(), lhsT=lap(), rhs=rhs.ap,
                               start=(i == 0), stop=(i == n - 1)),
                      reads=[rhs] + lbs, writes=[dst_ps])

        for tc in range(NT):
            t0 = tc * TC
            for k in range(KD):
                pr.op("sync", lambda h, k=k, t0=t0: h.dma_start(
                    out=xf[k].ap, in_=xT[128 * k:128 * (k + 1), t0:t0 + TC]),
                    writes=[xf[k]], dma=True)
                pr.op("scalar", lambda h, k=k: h.activation(
                    out=xb[k].ap, in_=xf[k].ap, func=AF.Copy),
                    reads=[xf[k]], writes=[xb[k]])

            for l in range(NL):
                xin_bf = xb
                def inproj_front(m):
                    wt = wpool[m % 2]
                    pr.op("sync", lambda h, l=l, m=m, wt=wt: h.dma_start(
                        out=wt.h[:, 0:KD * 128],
                        in_=sc_in[l, m].rearrange("p a b -> p (a b)")),
                        reads=[scb["sc_in"]], writes=[wt], dma=True)
                    ps = pool6[m % 6]
                    mm_acc(ps, lambda ps=ps: ps.ap,
                           [((lambda wt=wt, c=c: wt.h[:, 128 * c:128 * (c + 1)]),
                             xin_bf[c], [wt]) for c in range(KD)])
                    ue = u_ext[m % 4]
                    pr.op("scalar", lambda h, ue=ue, ps=ps: h.activation(
                        out=ue.h[:, 3:TC + 3], in_=ps.ap, func=AF.Copy),
                        reads=[ps], writes=[ue])
                def inproj_back(m):
                    ue = u_ext[m % 4]
                    pr.op("vector", lambda h, ue=ue, l=l, m=m: h.tensor_copy(
                        out=ue.h[:, 0:3], in_=cvtail[l][m].ap),
                        reads=[cvtail[l][m]], writes=[ue])
                    pr.op("vector", lambda h, ue=ue, l=l, m=m: h.tensor_copy(
                        out=cvtail[l][m].ap, in_=ue.h[:, TC:TC + 3]),
                        reads=[ue], writes=[cvtail[l][m]])
                    ca = conv_acc[m % 2]
                    pr.op("vector", lambda h, ca=ca, ue=ue, l=l, m=m:
                          h.tensor_scalar_mul(
                              out=ca.ap, in0=ue.h[:, 0:TC],
                              scalar1=cw[l].h[:, 4 * m:4 * m + 1]),
                          reads=[ue, cw[l]], writes=[ca])
                    for k in range(1, 4):
                        pr.op("vector", lambda h, ca=ca, ue=ue, l=l, m=m, k=k:
                              h.scalar_tensor_tensor(
                                  out=ca.ap, in0=ue.h[:, k:k + TC],
                                  scalar=cw[l].h[:, 4 * m + k:4 * m + k + 1],
                                  in1=ca.ap, op0=ALU.mult, op1=ALU.add),
                              reads=[ue, cw[l], ca], writes=[ca])
                    pr.op("scalar", lambda h, ca=ca, l=l, m=m: h.activation(
                        out=u_bf[m].ap, in_=ca.ap, func=AF.Silu,
                        bias=cb[l].h[:, m:m + 1]),
                        reads=[ca, cb[l]], writes=[u_bf[m]])
                for m in range(KE + 2):
                    if m < KE:
                        inproj_front(m)
                    if m >= 2:
                        inproj_back(m - 2)

                for gi, (dst, off) in enumerate([(dtr_bf, 0), (bmat_bf, 64),
                                                 (cmat_bf, 128)]):
                    ps = pool6[gi % 6]
                    mm_acc(ps, lambda ps=ps: ps.h[0:64, :],
                           [((lambda c=c, l=l, off=off:
                              wxp_b[l].h[:, 192 * c + off:192 * c + off + 64]),
                             u_bf[c], [wxp_b[l]]) for c in range(KE)])
                    pr.op("scalar", lambda h, ps=ps, dst=dst: h.activation(
                        out=dst.ap, in_=ps.h[0:64, :], func=AF.Copy),
                        reads=[ps], writes=[dst])

                def dt_front(m):
                    ps = pool6[m % 6]
                    pr.op("tensor", lambda h, ps=ps, l=l, m=m: h.matmul(
                        out=ps.ap, lhsT=wdt_b[l].h[:, 128 * m:128 * (m + 1)],
                        rhs=dtr_bf.ap, start=True, stop=True),
                        reads=[wdt_b[l], dtr_bf], writes=[ps])
                    s1 = sp1[m % 2]; s2 = sp2[m % 2]
                    pr.op("scalar", lambda h, ps=ps, s1=s1, l=l, m=m: h.activation(
                        out=s1.ap, in_=ps.ap, func=AF.Abs, bias=dtb[l].h[:, m:m + 1]),
                        reads=[ps, dtb[l]], writes=[s1])
                    pr.op("scalar", lambda h, s1=s1: h.activation(
                        out=s1.ap, in_=s1.ap, func=AF.Exp, scale=-1.0),
                        reads=[s1], writes=[s1])
                    pr.op("scalar", lambda h, s1=s1: h.activation(
                        out=s1.ap, in_=s1.ap, func=AF.Ln, bias=1.0),
                        reads=[s1], writes=[s1])
                    pr.op("scalar", lambda h, ps=ps, s2=s2, l=l, m=m: h.activation(
                        out=s2.ap, in_=ps.ap, func=AF.Relu, bias=dtb[l].h[:, m:m + 1]),
                        reads=[ps, dtb[l]], writes=[s2])
                def dt_back(m):
                    s1 = sp1[m % 2]; s2 = sp2[m % 2]
                    pr.op("vector", lambda h, s1=s1, s2=s2, m=m: h.tensor_add(
                        out=dt_bf[m].ap, in0=s1.ap, in1=s2.ap),
                        reads=[s1, s2], writes=[dt_bf[m]])
                    pr.op("vector", lambda h, m=m: h.tensor_mul(
                        out=du_bf[m].ap, in0=dt_bf[m].ap, in1=u_bf[m].ap),
                        reads=[dt_bf[m], u_bf[m]], writes=[du_bf[m]])
                for m in range(KE + 2):
                    if m >= 2:
                        dt_back(m - 2)
                    if m < KE:
                        dt_front(m)

                for half in range(2):
                    ms = list(range(8 * half, 8 * half + 8))
                    for mi, m in enumerate(ms):
                        pr.op("vector", lambda h, mi=mi, m=m, l=l: h.tensor_scalar_mul(
                            out=yac[mi].ap, in0=u_bf[m].ap,
                            scalar1=dsk[l].h[:, m:m + 1]),
                            reads=[u_bf[m], dsk[l]], writes=[yac[mi]])
                    items = [(s2, mi) for s2 in range(0, S, 2) for mi in range(8)]
                    NI = len(items)
                    bix = lambda s2: 2 * ((s2 // 2) % 2)

                    def emit_bcast(s2):
                        bi = bix(s2)
                        for ds in range(2):
                            s = s2 + ds
                            sl = sel[ds]
                            pr.op("sync", lambda h, sl=sl, s=s: h.dma_start(
                                out=sl.ap, in_=sel_all[s]),
                                reads=[selab], writes=[sl], dma=True)
                            bb = pbc[ds]; cc = pcc[ds]
                            pr.op("tensor", lambda h, bb=bb, sl=sl: h.matmul(
                                out=bb.ap, lhsT=sl.ap, rhs=bmat_bf.ap,
                                start=True, stop=True),
                                reads=[sl, bmat_bf], writes=[bb])
                            pr.op("tensor", lambda h, cc=cc, sl=sl: h.matmul(
                                out=cc.ap, lhsT=sl.ap, rhs=cmat_bf.ap,
                                start=True, stop=True),
                                reads=[sl, cmat_bf], writes=[cc])
                            pr.op("scalar", lambda h, t=bbs4[bi + ds], bb=bb:
                                  h.activation(out=t.ap, in_=bb.ap, func=AF.Copy),
                                  reads=[bb], writes=[bbs4[bi + ds]])
                            pr.op("scalar", lambda h, t=ccs4[bi + ds], cc=cc:
                                  h.activation(out=t.ap, in_=cc.ap, func=AF.Copy),
                                  reads=[cc], writes=[ccs4[bi + ds]])

                    emit_bcast(0)
                    G3 = 2
                    NSG = NI // G3
                    def sg_items(j):
                        return [(k, items[k]) for k in
                                range(j * G3, min(NI, (j + 1) * G3)) if k < NI]
                    for j in range(NSG + 2):
                        for (k, (s2, mi)) in sg_items(j):
                            m = ms[mi]
                            if mi == 0 and s2 + 2 < S:
                                emit_bcast(s2 + 2)
                            for ds in range(2):
                                s = s2 + ds
                                da = da_t[(k * 2 + ds) % 8]
                                pr.op("scalar", lambda h, da=da, l=l, m=m, s=s:
                                      h.activation(
                                          out=da.ap, in_=dt_bf[m].ap, func=AF.Exp,
                                          scale=aneg[l].h[:, m * S + s:m * S + s + 1]),
                                      reads=[dt_bf[m], aneg[l]], writes=[da])
                        for (k, (s2, mi)) in sg_items(j):
                            m = ms[mi]
                            for ds in range(2):
                                bt = b_t[(k * 2 + ds) % 8]
                                bbuf = bbs4[bix(s2) + ds]
                                pr.op("vector", lambda h, bt=bt, m=m, bbuf=bbuf:
                                      h.tensor_mul(out=bt.ap, in0=du_bf[m].ap,
                                                   in1=bbuf.ap),
                                      reads=[du_bf[m], bbuf], writes=[bt])
                        if sg_items(j):
                            kw = da_t[(sg_items(j)[-1][0] * 2 + 1) % 8]
                            pr.op("tensor", lambda h, kw=kw: h.matmul(
                                out=ps1.h[0:1, 0:min(128, TC)], lhsT=ones_col.ap,
                                rhs=kw.h[:, 0:min(128, TC)], start=True, stop=True),
                                reads=[ones_col, kw], writes=[ps1])
                        if 2 <= j:
                            for (k, (s2, mi)) in sg_items(j - 2):
                                for ds in range(2):
                                    ht = h_t[(k * 2 + ds) % 8]
                                    pd = pd_t[(k * 2 + ds) % 4]
                                    cbuf = ccs4[bix(s2) + ds]
                                    pr.op("vector", lambda h, pd=pd, ht=ht, cbuf=cbuf:
                                          h.tensor_mul(out=pd.ap, in0=ht.ap,
                                                       in1=cbuf.ap),
                                          reads=[ht, cbuf], writes=[pd])
                            for (k, (s2, mi)) in sg_items(j - 2):
                                pp = ppd_t[k % 2]
                                pa = pd_t[(k * 2) % 4]; pb = pd_t[(k * 2 + 1) % 4]
                                pr.op("vector", lambda h, pp=pp, pa=pa, pb=pb:
                                      h.tensor_add(out=pp.ap, in0=pa.ap, in1=pb.ap),
                                      reads=[pa, pb], writes=[pp])
                            kw = da_t[(sg_items(j - 2)[-1][0] * 2) % 8]
                            pr.op("tensor", lambda h, kw=kw: h.matmul(
                                out=ps1.h[0:1, 0:min(128, TC)], lhsT=ones_col.ap,
                                rhs=kw.h[:, 0:min(128, TC)], start=True, stop=True),
                                reads=[ones_col, kw], writes=[ps1])
                            for (k, (s2, mi)) in sg_items(j - 2):
                                pp = ppd_t[k % 2]
                                ya = yac[mi]
                                pr.op("vector", lambda h, ya=ya, pp=pp: h.tensor_add(
                                    out=ya.ap, in0=ya.ap, in1=pp.ap),
                                    reads=[pp, ya], writes=[ya])
                            for (k, (s2, mi)) in sg_items(j - 2):
                                m = ms[mi]
                                for ds in range(2):
                                    ht = h_t[(k * 2 + ds) % 8]
                                    s = s2 + ds
                                    hc = hcol[l][m][s]
                                    col = m * S + s
                                    pr.op("scalar", lambda h, ht=ht, hc=hc, col=col:
                                          h.activation(out=hc.h[:, col:col + 1],
                                                       in_=ht.h[:, TC - 1:TC],
                                                       func=AF.Copy),
                                          reads=[ht], writes=[hc])
                        if 1 <= j:
                            for (k, (s2, mi)) in sg_items(j - 1):
                                m = ms[mi]
                                for ds in range(2):
                                    s = s2 + ds
                                    da = da_t[(k * 2 + ds) % 8]
                                    bt = b_t[(k * 2 + ds) % 8]
                                    ht = h_t[(k * 2 + ds) % 8]
                                    hc = hcol[l][m][s]
                                    col = m * S + s
                                    pr.op("vector",
                                          lambda h, ht=ht, da=da, bt=bt, hc=hc,
                                          col=col:
                                          h.tensor_tensor_scan(
                                              out=ht.ap, data0=da.ap, data1=bt.ap,
                                              initial=hc.h[:, col:col + 1],
                                              op0=ALU.mult, op1=ALU.add),
                                          reads=[da, bt, hc], writes=[ht])
                            kw = da_t[((j - 1) * 4 + 1) % 8]
                            pr.op("tensor", lambda h, kw=kw: h.matmul(
                                out=ps1.h[0:1, 0:min(128, TC)], lhsT=ones_col.ap,
                                rhs=kw.h[:, 0:min(128, TC)], start=True, stop=True),
                                reads=[ones_col, kw], writes=[ps1])
                    def z_front(mi, m):
                        wt = wpool[m % 2]
                        pr.op("sync", lambda h, l=l, m=m, wt=wt: h.dma_start(
                            out=wt.h[:, 0:KD * 128],
                            in_=sc_in[l, KE + m].rearrange("p a b -> p (a b)")),
                            reads=[scb["sc_in"]], writes=[wt], dma=True)
                        ps = pool6[m % 6]
                        mm_acc(ps, lambda ps=ps: ps.ap,
                               [((lambda wt=wt, c=c: wt.h[:, 128 * c:128 * (c + 1)]),
                                 xin_bf[c], [wt]) for c in range(KD)])
                        zs = zs_t[mi % 2]
                        pr.op("scalar", lambda h, zs=zs, ps=ps: h.activation(
                            out=zs.ap, in_=ps.ap, func=AF.Silu),
                            reads=[ps], writes=[zs])
                    def z_back(mi, m):
                        ya = yac[mi]
                        zs = zs_t[mi % 2]
                        pr.op("vector", lambda h, ya=ya, zs=zs, m=m: h.tensor_mul(
                            out=du_bf[m].ap, in0=ya.ap, in1=zs.ap),
                            reads=[ya, zs], writes=[du_bf[m]])
                    for mi in range(10):
                        if mi >= 2:
                            z_back(mi - 2, ms[mi - 2])
                        if mi < 8:
                            z_front(mi, ms[mi])

                def op_front(k):
                    wt = wpool[k % 2]
                    pr.op("sync", lambda h, l=l, k=k, wt=wt: h.dma_start(
                        out=wt.h[:, 0:KE * 128],
                        in_=sc_out[l, k].rearrange("p a b -> p (a b)")),
                        reads=[scb["sc_out"]], writes=[wt], dma=True)
                    ps = pool6[k % 6]
                    mm_acc(ps, lambda ps=ps: ps.ap,
                           [((lambda wt=wt, c=c: wt.h[:, 128 * c:128 * (c + 1)]),
                             du_bf[c], [wt]) for c in range(KE)])
                def op_back(k):
                    ps = pool6[k % 6]
                    if l == 0:
                        pr.op("scalar", lambda h, ps=ps, k=k: h.activation(
                            out=xb[k].ap, in_=ps.ap, func=AF.Copy),
                            reads=[ps], writes=[xb[k]])
                    else:
                        pr.op("vector", lambda h, ps=ps, k=k: h.tensor_add(
                            out=xf[k].ap, in0=ps.ap, in1=xf[k].ap),
                            reads=[ps, xf[k]], writes=[xf[k]])
                for k in range(KD + 2):
                    if k < KD:
                        op_front(k)
                    if k >= 2:
                        op_back(k - 2)
            def layernorm(src_f, dst_f, dst_b):
                for k in range(KD):
                    pr.op("tensor", lambda h, k=k: h.matmul(
                        out=ps1.ap, lhsT=ones_col.ap, rhs=src_f[k].ap,
                        start=(k == 0), stop=(k == KD - 1)),
                        reads=[ones_col, src_f[k]], writes=[ps1])
                for k in range(KD):
                    sq = sq_t[k % 2]
                    pr.op("scalar", lambda h, k=k, sq=sq: h.activation(
                        out=sq.ap, in_=src_f[k].ap, func=AF.Square),
                        reads=[src_f[k]], writes=[sq])
                    pr.op("tensor", lambda h, k=k, sq=sq: h.matmul(
                        out=ps2.ap, lhsT=ones_col.ap, rhs=sq.ap,
                        start=(k == 0), stop=(k == KD - 1)),
                        reads=[ones_col, sq], writes=[ps2])
                pr.op("vector", lambda h: h.tensor_scalar_mul(
                    out=row_a.ap, in0=ps1.ap, scalar1=1.0 / D),
                    reads=[ps1], writes=[row_a])
                pr.op("vector", lambda h: h.tensor_mul(
                    out=row_b.ap, in0=row_a.ap, in1=row_a.ap),
                    reads=[row_a], writes=[row_b])
                pr.op("vector", lambda h: h.scalar_tensor_tensor(
                    out=row_b.ap, in0=ps2.ap, scalar=1.0 / D, in1=row_b.ap,
                    op0=ALU.mult, op1=ALU.subtract),
                    reads=[ps2, row_b], writes=[row_b])
                pr.op("scalar", lambda h: h.activation(
                    out=row_b.ap, in_=row_b.ap, func=AF.Ln, bias=epsrow.ap),
                    reads=[row_b, epsrow], writes=[row_b])
                pr.op("scalar", lambda h: h.activation(
                    out=row_b.ap, in_=row_b.ap, func=AF.Exp, scale=-0.5),
                    reads=[row_b], writes=[row_b])
                pr.op("vector", lambda h: h.tensor_mul(
                    out=row_c.ap, in0=row_a.ap, in1=row_b.ap),
                    reads=[row_a, row_b], writes=[row_c])
                pr.op("tensor", lambda h: h.matmul(
                    out=pbc[0].ap, lhsT=ones_row.ap, rhs=row_b.ap,
                    start=True, stop=True),
                    reads=[ones_row, row_b], writes=[pbc[0]])
                pr.op("tensor", lambda h: h.matmul(
                    out=pcc[0].ap, lhsT=ones_row.ap, rhs=row_c.ap,
                    start=True, stop=True),
                    reads=[ones_row, row_c], writes=[pcc[0]])
                for k in range(KD):
                    pr.op("vector", lambda h, k=k: h.tensor_mul(
                        out=dst_f[k].ap, in0=src_f[k].ap, in1=pbc[0].ap),
                        reads=[src_f[k], pbc[0]], writes=[dst_f[k]])
                    pr.op("vector", lambda h, k=k: h.tensor_sub(
                        out=dst_f[k].ap, in0=dst_f[k].ap, in1=pcc[0].ap),
                        reads=[dst_f[k], pcc[0]], writes=[dst_f[k]])
                    if dst_b is not None:
                        pr.op("scalar", lambda h, k=k: h.activation(
                            out=dst_b[k].ap, in_=dst_f[k].ap, func=AF.Copy),
                            reads=[dst_f[k]], writes=[dst_b[k]])

            layernorm(xf, xf, xb)

            hbt = u_bf + dt_bf[:KH - KE]
            def ffn_front(m):
                wt = wpool[m % 2]
                pr.op("sync", lambda h, m=m, wt=wt: h.dma_start(
                    out=wt.h[:, 0:2 * KD * 128],
                    in_=sc_gu[m].rearrange("p a b -> p (a b)")),
                    reads=[scb["sc_gu"]], writes=[wt], dma=True)
                psg = pool6[(2 * m) % 6]; psu = pool6[(2 * m + 1) % 6]
                mm_acc(psg, lambda psg=psg: psg.ap,
                       [((lambda wt=wt, c=c: wt.h[:, 128 * c:128 * (c + 1)]),
                         xb[c], [wt]) for c in range(KD)])
                mm_acc(psu, lambda psu=psu: psu.ap,
                       [((lambda wt=wt, c=c: wt.h[:, 128 * (KD + c):128 * (KD + c + 1)]),
                         xb[c], [wt]) for c in range(KD)])
                gs = zs_t[m % 2]
                pr.op("scalar", lambda h, gs=gs, psg=psg: h.activation(
                    out=gs.ap, in_=psg.ap, func=AF.Silu, scale=1.0 / 127.0),
                    reads=[psg], writes=[gs])
            def ffn_back(m):
                psu = pool6[(2 * m + 1) % 6]
                gs = zs_t[m % 2]
                pr.op("vector", lambda h, m=m, gs=gs, psu=psu: h.scalar_tensor_tensor(
                    out=hbt[m].ap, in0=psu.ap, scalar=1.0 / 127.0, in1=gs.ap,
                    op0=ALU.mult, op1=ALU.mult),
                    reads=[psu, gs], writes=[hbt[m]])
            for m in range(KH + 1):
                if m >= 1:
                    ffn_back(m - 1)
                if m < KH:
                    ffn_front(m)
            for k in range(KD):
                wta, wtb = wpool[0], wpool[1]
                pr.op("sync", lambda h, k=k, wta=wta: h.dma_start(
                    out=wta.h[:, 0:12 * 128],
                    in_=sc_d[k, :, 0:12, :].rearrange("p a b -> p (a b)")),
                    reads=[scb["sc_d"]], writes=[wta], dma=True)
                pr.op("sync", lambda h, k=k, wtb=wtb: h.dma_start(
                    out=wtb.h[:, 0:12 * 128],
                    in_=sc_d[k, :, 12:24, :].rearrange("p a b -> p (a b)")),
                    reads=[scb["sc_d"]], writes=[wtb], dma=True)
                ps = pool6[k % 6]
                mm_acc(ps, lambda ps=ps: ps.ap,
                       [((lambda wt=(wta if c < 12 else wtb), c=c:
                          wt.h[:, 128 * (c % 12):128 * (c % 12 + 1)]),
                         hbt[c], [wta if c < 12 else wtb]) for c in range(KH)])
                pr.op("vector", lambda h, ps=ps, k=k: h.scalar_tensor_tensor(
                    out=xf[k].ap, in0=ps.ap, scalar=1.0 / 127.0, in1=xf[k].ap,
                    op0=ALU.mult, op1=ALU.add),
                    reads=[ps, xf[k]], writes=[xf[k]])
            layernorm(xf, xf, None)

            for k in range(KD):
                pr.op("gpsimd", lambda h, k=k, t0=t0: h.dma_start(
                    out=outT[128 * k:128 * (k + 1), t0:t0 + TC], in_=xf[k].ap),
                    reads=[xf[k]], dma=True)

        pr.wait("gpsimd", pr.barrier_tok("dma_gpsimd"))
        pr.emit(nc, es)
    return nc



def _prep_inputs(L0_attn_x, L0_attn_in_proj_w, L0_attn_conv_w, L0_attn_conv_b,
                 L0_attn_x_proj_w, L0_attn_dt_proj_w, L0_attn_dt_proj_b,
                 L0_attn_A_log, L0_attn_D_skip, L0_attn_out_proj_w,
                 L0_ffn_w_gate, L0_ffn_w_up, L0_ffn_w_down):
    C = lambda a: np.ascontiguousarray(a)
    A = lambda a: np.asarray(a)
    shared = {
        "w_in": C(np.transpose(A(L0_attn_in_proj_w), (0, 2, 1))),
        "w_xp": C(np.transpose(A(L0_attn_x_proj_w), (0, 2, 1))),
        "w_dt": C(np.transpose(A(L0_attn_dt_proj_w), (0, 2, 1))),
        "w_out": C(np.transpose(A(L0_attn_out_proj_w), (0, 2, 1))),
        "cw_r": C(A(L0_attn_conv_w).reshape(NL, KE, 128, 4)
                  .transpose(0, 2, 1, 3).reshape(NL, 128, KE * 4)),
        "cb_r": C(A(L0_attn_conv_b).reshape(NL, KE, 128).transpose(0, 2, 1)),
        "dtb_r": C(A(L0_attn_dt_proj_b).reshape(NL, KE, 128).transpose(0, 2, 1)),
        "dsk_r": C(A(L0_attn_D_skip).reshape(NL, KE, 128).transpose(0, 2, 1)),
        "alog_r": C(A(L0_attn_A_log).reshape(NL, KE, 128, S)
                    .transpose(0, 2, 1, 3).reshape(NL, 128, KE * S)),
        "sel_all": np.repeat(np.eye(S, dtype=np.float32)[:, :, None]
                             .astype(ml_dtypes.bfloat16), 128, axis=2),
        "wg": C(A(L0_ffn_w_gate).T),
        "wu": C(A(L0_ffn_w_up).T),
        "wd": C(A(L0_ffn_w_down).T),
    }
    x = A(L0_attn_x)
    per_core = [dict(shared, xT=C(x[i].T)) for i in range(x.shape[0])]
    return per_core


def kernel(**inputs):
    per_core = _prep_inputs(**inputs)
    nc = build_core()
    res = run_bass_kernel_spmd(nc, per_core, core_ids=list(range(8)))
    outs = [np.ascontiguousarray(r["outT"].T) for r in res.results]
    return np.stack(outs, 0).astype(np.float32)


# version 24
# speedup vs baseline: 1.1217x; 1.0356x over previous
import contextlib
import numpy as np
import ml_dtypes
import concourse.bass as bass
import concourse.mybir as mybir
from concourse.bass_utils import run_bass_kernel_spmd

F32 = mybir.dt.float32
BF16 = mybir.dt.bfloat16
I32 = mybir.dt.int32
AF = mybir.ActivationFunctionType
ALU = mybir.AluOpType


class Tok:
    __slots__ = ("k", "v")
    def __init__(self, k, v):
        self.k = k
        self.v = v


class Buf:
    __slots__ = ("h", "w", "r", "name")
    def __init__(self, h, name=""):
        self.h = h
        self.name = name or getattr(h, "name", "")
        self.w = None
        self.r = []
    def __getitem__(self, idx):
        return self.h[idx]
    @property
    def ap(self):
        return self.h[:]


ENGINES = ("tensor", "vector", "scalar", "gpsimd", "sync")


class Prog:
    def __init__(self):
        self.items = {e: [] for e in ENGINES}
        self.counts = {}
        for e in ENGINES:
            self.counts[e] = 0
            self.counts["dma_" + e] = 0
        self.wm = {e: {} for e in ENGINES}
        self.pending = {"dma_" + e: [] for e in ENGINES}

    def _fence(self, key):
        plist = self.pending[key]
        if not plist:
            return
        total = self.counts[key]
        for t in plist:
            t.v = total
        plist.clear()
        eng = key[4:]
        self.items[eng].append(([(key, total)], None, None, 0))
        wm = self.wm[eng]
        if wm.get(key, 0) < total:
            wm[key] = total

    def op(self, eng, fn, reads=(), writes=(), dma=False):
        toks = []
        for b in reads:
            if b.w:
                toks.append(b.w)
        for b in writes:
            if b.w:
                toks.append(b.w)
            toks.extend(b.r)
        for t in toks:
            if t.v is None:
                self._fence(t.k)
        best = {}
        for t in toks:
            if t.v > best.get(t.k, -1):
                best[t.k] = t.v
        wm = self.wm[eng]
        waits = []
        for k in sorted(best):
            v = best[k]
            if wm.get(k, 0) < v:
                waits.append((k, v))
                wm[k] = v
        key = ("dma_" + eng) if dma else eng
        step = 16 if dma else 1
        self.counts[key] += step
        if dma:
            tok = Tok(key, None)
            self.pending[key].append(tok)
        else:
            tok = Tok(key, self.counts[key])
        self.items[eng].append((waits, fn, key, step))
        for b in writes:
            b.w = tok
            b.r = []
        for b in reads:
            b.r.append(tok)
        return tok

    def barrier_tok(self, key):
        if key.startswith("dma_"):
            self._fence(key)
        return Tok(key, self.counts[key])

    def wait(self, eng, tok):
        if tok.v is None:
            self._fence(tok.k)
        wm = self.wm[eng]
        if wm.get(tok.k, 0) >= tok.v:
            return
        wm[tok.k] = tok.v
        self.items[eng].append(([(tok.k, tok.v)], None, None, 0))

    def emit(self, nc, es):
        used = set(k for lst in self.items.values()
                   for (ws, _f, _k, _s) in lst for (k, _v) in ws)
        used |= set(k for k, c in self.counts.items() if c > 0)
        sems = {k: es.enter_context(nc.semaphore("sq_" + k)) for k in sorted(used)}
        block = es.enter_context(nc.Block())
        for e in ENGINES:
            lst = self.items[e]
            if not lst:
                continue
            def make(lst):
                def body(h):
                    for (waits, fn, key, step) in lst:
                        for (k, v) in waits:
                            h.wait_ge(sems[k], v)
                        if fn is not None:
                            fn(h).then_inc(sems[key], step)
                return body
            getattr(block, e)(make(lst))



D = 1024
E = 2048
S = 64
RK = 64
HID = 3072
NL = 2
KD = D // 128
KE = E // 128
KH = HID // 128
EPS = 1e-5


def build_core(L=4096, TC=512):
    NT = L // TC
    nc = bass.Bass("TRN2", target_bir_lowering=False)

    xT = nc.dram_tensor("xT", [D, L], F32, kind="ExternalInput").ap()
    outT = nc.dram_tensor("outT", [D, L], F32, kind="ExternalOutput").ap()
    w_in = nc.dram_tensor("w_in", [NL, D, 2 * E], F32, kind="ExternalInput").ap()
    w_xp = nc.dram_tensor("w_xp", [NL, E, 192], F32, kind="ExternalInput").ap()
    w_dt = nc.dram_tensor("w_dt", [NL, RK, E], F32, kind="ExternalInput").ap()
    w_out = nc.dram_tensor("w_out", [NL, E, D], F32, kind="ExternalInput").ap()
    cw_r = nc.dram_tensor("cw_r", [NL, 128, KE * 4], F32, kind="ExternalInput").ap()
    cb_r = nc.dram_tensor("cb_r", [NL, 128, KE], F32, kind="ExternalInput").ap()
    dtb_r = nc.dram_tensor("dtb_r", [NL, 128, KE], F32, kind="ExternalInput").ap()
    dsk_r = nc.dram_tensor("dsk_r", [NL, 128, KE], F32, kind="ExternalInput").ap()
    alog_r = nc.dram_tensor("alog_r", [NL, 128, KE * S], F32, kind="ExternalInput").ap()
    sel_all = nc.dram_tensor("sel_all", [S, S, 128], BF16, kind="ExternalInput").ap()
    wg = nc.dram_tensor("wg", [D, HID], I32, kind="ExternalInput").ap()
    wu = nc.dram_tensor("wu", [D, HID], I32, kind="ExternalInput").ap()
    wd = nc.dram_tensor("wd", [HID, D], I32, kind="ExternalInput").ap()

    sc_in = nc.dram_tensor("sc_in", [NL, 2 * KE, 128, KD, 128], BF16,
                           kind="Internal").ap()
    sc_out = nc.dram_tensor("sc_out", [NL, KD, 128, KE, 128], BF16,
                            kind="Internal").ap()
    sc_gu = nc.dram_tensor("sc_gu", [KH, 128, 2 * KD, 128], BF16,
                           kind="Internal").ap()
    sc_d = nc.dram_tensor("sc_d", [KD, 128, KH, 128], BF16, kind="Internal").ap()

    es = contextlib.ExitStack()
    pr = Prog()
    with es:
        def T(n, shp, dt=F32):
            return Buf(es.enter_context(nc.sbuf_tensor(n, shp, dt)), n)
        def PT(n, shp):
            return Buf(es.enter_context(nc.psum_tensor(n, shp, F32)), n)

        ones_bf = T("ones_bf", [1, 128], BF16)
        ones_col = T("ones_col", [128, 1], F32)
        ones_row = T("ones_row", [1, 128], F32)
        epsrow = T("epsrow", [1, 1], F32)
        pr.op("vector", lambda h: h.memset(ones_bf.ap, 1.0), writes=[ones_bf])
        pr.op("vector", lambda h: h.memset(ones_col.ap, 1.0), writes=[ones_col])
        pr.op("vector", lambda h: h.memset(ones_row.ap, 1.0), writes=[ones_row])
        pr.op("vector", lambda h: h.memset(epsrow.ap, EPS), writes=[epsrow])

        cw = [T(f"cw{l}", [128, KE * 4]) for l in range(NL)]
        cb = [T(f"cb{l}", [128, KE]) for l in range(NL)]
        dtb = [T(f"dtb{l}", [128, KE]) for l in range(NL)]
        dsk = [T(f"dsk{l}", [128, KE]) for l in range(NL)]
        aneg = [T(f"aneg{l}", [128, KE * S]) for l in range(NL)]
        wxp_b = [T(f"wxp{l}", [128, KE * 192], BF16) for l in range(NL)]
        wdt_b = [T(f"wdtb{l}", [RK, E], BF16) for l in range(NL)]

        for l in range(NL):
            for (dst, src) in [(cw[l], cw_r[l]), (cb[l], cb_r[l]),
                               (dtb[l], dtb_r[l]), (dsk[l], dsk_r[l]),
                               (aneg[l], alog_r[l])]:
                pr.op("sync", lambda h, dst=dst, src=src: h.dma_start(
                    out=dst.ap, in_=src), writes=[dst], dma=True)
            pr.op("scalar", lambda h, l=l: h.activation(
                out=aneg[l].ap, in_=aneg[l].ap, func=AF.Exp),
                reads=[aneg[l]], writes=[aneg[l]])
            pr.op("vector", lambda h, l=l: h.tensor_scalar_mul(
                out=aneg[l].ap, in0=aneg[l].ap, scalar1=-1.0),
                reads=[aneg[l]], writes=[aneg[l]])

        stg_f = T("stg_f", [128, 256], F32)
        stg_i = Buf(es.enter_context(nc.sbuf_tensor("stg_i", [128, 256], I32)), "stg_i")
        stg_b = T("stg_b", [128, 256], BF16)
        scb = {"sc_in": Buf(None, "sc_in"), "sc_out": Buf(None, "sc_out"),
               "sc_gu": Buf(None, "sc_gu"), "sc_d": Buf(None, "sc_d")}

        for l in range(NL):
            for c in range(KE):
                pr.op("sync", lambda h, l=l, c=c: h.dma_start(
                    out=stg_f.h[:, 0:192], in_=w_xp[l, 128 * c:128 * (c + 1), :]),
                    writes=[stg_f], dma=True)
                pr.op("gpsimd", lambda h, l=l, c=c: h.tensor_copy(
                    out=wxp_b[l].h[:, 192 * c:192 * (c + 1)], in_=stg_f.h[:, 0:192]),
                    reads=[stg_f], writes=[wxp_b[l]])
            for g in range(8):
                pr.op("sync", lambda h, l=l, g=g: h.dma_start(
                    out=stg_f.h[0:RK, :], in_=w_dt[l, :, 256 * g:256 * (g + 1)]),
                    writes=[stg_f], dma=True)
                pr.op("gpsimd", lambda h, l=l, g=g: h.tensor_copy(
                    out=wdt_b[l].h[:, 256 * g:256 * (g + 1)], in_=stg_f.h[0:RK, :]),
                    reads=[stg_f], writes=[wdt_b[l]])

        def run_prepass(get_rows, get_dst, n_k, n_m, is_int):
            GM = 2
            stg_in = stg_i if is_int else stg_f
            for c in range(n_k):
                rows = get_rows(c)
                for g0 in range(0, n_m, GM):
                    gm = min(GM, n_m - g0)
                    pr.op("sync", lambda h, rows=rows, g0=g0, gm=gm, stg_in=stg_in:
                          h.dma_start(out=stg_in.h[:, 0:gm * 128],
                                      in_=rows[:, 128 * g0:128 * (g0 + gm)]),
                          writes=[stg_in], dma=True)
                    pr.op("gpsimd", lambda h, gm=gm, stg_in=stg_in: h.tensor_copy(
                        out=stg_b.h[:, 0:gm * 128], in_=stg_in.h[:, 0:gm * 128]),
                        reads=[stg_in], writes=[stg_b])
                    dst = get_dst(c, g0, gm)
                    pr.op("gpsimd", lambda h, dst=dst, gm=gm: h.dma_start(
                        out=dst.rearrange("a p b -> p a b"),
                        in_=stg_b.h[:, 0:gm * 128]
                        .rearrange("p (a b) -> p a b", a=gm)),
                        reads=[stg_b], writes=[scb_cur[0]], dma=True)

        for l in range(NL):
            scb_cur = [scb["sc_in"]]
            run_prepass(lambda c, l=l: w_in[l, 128 * c:128 * (c + 1), :],
                        lambda c, g0, gm, l=l: sc_in[l, g0:g0 + gm, :, c, :],
                        KD, 2 * KE, False)
            scb_cur = [scb["sc_out"]]
            run_prepass(lambda c, l=l: w_out[l, 128 * c:128 * (c + 1), :],
                        lambda c, g0, gm, l=l: sc_out[l, g0:g0 + gm, :, c, :],
                        KE, KD, False)
        scb_cur = [scb["sc_gu"]]
        run_prepass(lambda c: wg[128 * c:128 * (c + 1), :],
                    lambda c, g0, gm: sc_gu[g0:g0 + gm, :, c, :], KD, KH, True)
        run_prepass(lambda c: wu[128 * c:128 * (c + 1), :],
                    lambda c, g0, gm: sc_gu[g0:g0 + gm, :, KD + c, :], KD, KH, True)
        scb_cur = [scb["sc_d"]]
        run_prepass(lambda c: wd[128 * c:128 * (c + 1), :],
                    lambda c, g0, gm: sc_d[g0:g0 + gm, :, c, :], KH, KD, True)

        xf = [T(f"xf{k}", [128, TC]) for k in range(KD)]
        xb = [T(f"xb{k}", [128, TC], BF16) for k in range(KD)]
        u_ext = [T(f"uext{i}", [128, TC + 3], BF16) for i in range(4)]
        u_bf = [T(f"ubf{m}", [128, TC], BF16) for m in range(KE)]
        dt_bf = [T(f"dtbf{m}", [128, TC], BF16) for m in range(KE)]
        du_bf = [T(f"dubf{m}", [128, TC], BF16) for m in range(KE)]
        conv_acc = [T(f"cacc{i}", [128, TC]) for i in range(2)]
        sp1 = [T(f"sp1_{i}", [128, TC], BF16) for i in range(2)]
        sp2 = [T(f"sp2_{i}", [128, TC], BF16) for i in range(2)]
        dtr_bf = T("dtr_bf", [RK, TC], BF16)
        bmat_bf = T("bmat_bf", [S, TC], BF16)
        cmat_bf = T("cmat_bf", [S, TC], BF16)
        da_t = [T(f"da{i}", [128, TC]) for i in range(8)]
        b_t = [T(f"b{i}", [128, TC], BF16) for i in range(8)]
        h_t = [T(f"h{i}", [128, TC], BF16) for i in range(8)]
        pd_t = [T(f"pd{i}", [128, TC], BF16) for i in range(4)]
        ppd_t = [T(f"ppd{i}", [128, TC], BF16) for i in range(2)]
        pdh_t = [sp1[0], sp1[1], sp2[0], sp2[1]] + u_ext
        bbs4 = [T(f"bbs{i}", [128, TC], BF16) for i in range(4)]
        ccs4 = [T(f"ccs{i}", [128, TC], BF16) for i in range(4)]
        yac = [T(f"yac{m}", [128, TC]) for m in range(8)]
        zs_t = [T(f"zs{i}", [128, TC]) for i in range(2)]
        sq_t = [T("sq_0", [128, TC])] * 2
        row_a = T("row_a", [1, TC])
        row_b = T("row_b", [1, TC])
        row_c = T("row_c", [1, TC])

        cvtail = [[T(f"cvt{l}_{m}", [128, 3], BF16) for m in range(KE)]
                  for l in range(NL)]
        hcar = [T(f"hcar{l}", [128, KE * S], BF16) for l in range(NL)]
        for l in range(NL):
            for m in range(KE):
                pr.op("vector", lambda h, l=l, m=m: h.memset(cvtail[l][m].ap, 0.0),
                      writes=[cvtail[l][m]])
            pr.op("vector", lambda h, l=l: h.memset(hcar[l].ap, 0.0),
                  writes=[hcar[l]])
        hcol = [[[Buf(hcar[l].h, f"hc{l}_{m}_{s}") for s in range(S)]
                 for m in range(KE)] for l in range(NL)]
        for l in range(NL):
            for m in range(KE):
                for s in range(S):
                    hcol[l][m][s].w = hcar[l].w

        wpool = [T(f"wpool{i}", [128, KE * 128], BF16) for i in range(2)]

        sel = [T(f"sel{i}", [S, 128], BF16) for i in range(2)]
        selab = Buf(None, "sel_all")

        pm = [PT(f"pm{i}", [128, TC]) for i in range(2)]
        pbc = [PT(f"pbc{i}", [128, TC]) for i in range(2)]
        pcc = [PT(f"pcc{i}", [128, TC]) for i in range(2)]
        pool6 = pm + pbc + pcc
        ps1 = PT("ps1", [1, TC])
        ps2 = PT("ps2", [1, TC])

        def mm_acc(dst_ps, out_ap_fn, pairs):
            n = len(pairs)
            for i, (lap, rhs, lbs) in enumerate(pairs):
                pr.op("tensor", lambda h, lap=lap, rhs=rhs, i=i:
                      h.matmul(out=out_ap_fn(), lhsT=lap(), rhs=rhs.ap,
                               start=(i == 0), stop=(i == n - 1)),
                      reads=[rhs] + lbs, writes=[dst_ps])

        for tc in range(NT):
            t0 = tc * TC
            for k in range(KD):
                pr.op("sync", lambda h, k=k, t0=t0: h.dma_start(
                    out=xf[k].ap, in_=xT[128 * k:128 * (k + 1), t0:t0 + TC]),
                    writes=[xf[k]], dma=True)
                pr.op("scalar", lambda h, k=k: h.activation(
                    out=xb[k].ap, in_=xf[k].ap, func=AF.Copy),
                    reads=[xf[k]], writes=[xb[k]])

            for l in range(NL):
                xin_bf = xb
                def inproj_front(m):
                    wt = wpool[m % 2]
                    pr.op("sync", lambda h, l=l, m=m, wt=wt: h.dma_start(
                        out=wt.h[:, 0:KD * 128],
                        in_=sc_in[l, m].rearrange("p a b -> p (a b)")),
                        reads=[scb["sc_in"]], writes=[wt], dma=True)
                    ps = pool6[m % 6]
                    mm_acc(ps, lambda ps=ps: ps.ap,
                           [((lambda wt=wt, c=c: wt.h[:, 128 * c:128 * (c + 1)]),
                             xin_bf[c], [wt]) for c in range(KD)])
                    ue = u_ext[m % 4]
                    pr.op("scalar", lambda h, ue=ue, ps=ps: h.activation(
                        out=ue.h[:, 3:TC + 3], in_=ps.ap, func=AF.Copy),
                        reads=[ps], writes=[ue])
                def inproj_back(m):
                    ue = u_ext[m % 4]
                    pr.op("vector", lambda h, ue=ue, l=l, m=m: h.tensor_copy(
                        out=ue.h[:, 0:3], in_=cvtail[l][m].ap),
                        reads=[cvtail[l][m]], writes=[ue])
                    pr.op("vector", lambda h, ue=ue, l=l, m=m: h.tensor_copy(
                        out=cvtail[l][m].ap, in_=ue.h[:, TC:TC + 3]),
                        reads=[ue], writes=[cvtail[l][m]])
                    ca = conv_acc[m % 2]
                    pr.op("vector", lambda h, ca=ca, ue=ue, l=l, m=m:
                          h.tensor_scalar_mul(
                              out=ca.ap, in0=ue.h[:, 0:TC],
                              scalar1=cw[l].h[:, 4 * m:4 * m + 1]),
                          reads=[ue, cw[l]], writes=[ca])
                    for k in range(1, 4):
                        pr.op("vector", lambda h, ca=ca, ue=ue, l=l, m=m, k=k:
                              h.scalar_tensor_tensor(
                                  out=ca.ap, in0=ue.h[:, k:k + TC],
                                  scalar=cw[l].h[:, 4 * m + k:4 * m + k + 1],
                                  in1=ca.ap, op0=ALU.mult, op1=ALU.add),
                              reads=[ue, cw[l], ca], writes=[ca])
                    pr.op("scalar", lambda h, ca=ca, l=l, m=m: h.activation(
                        out=u_bf[m].ap, in_=ca.ap, func=AF.Silu,
                        bias=cb[l].h[:, m:m + 1]),
                        reads=[ca, cb[l]], writes=[u_bf[m]])
                for m in range(KE + 2):
                    if m < KE:
                        inproj_front(m)
                    if m >= 2:
                        inproj_back(m - 2)

                for gi, (dst, off) in enumerate([(dtr_bf, 0), (bmat_bf, 64),
                                                 (cmat_bf, 128)]):
                    ps = pool6[gi % 6]
                    mm_acc(ps, lambda ps=ps: ps.h[0:64, :],
                           [((lambda c=c, l=l, off=off:
                              wxp_b[l].h[:, 192 * c + off:192 * c + off + 64]),
                             u_bf[c], [wxp_b[l]]) for c in range(KE)])
                    pr.op("scalar", lambda h, ps=ps, dst=dst: h.activation(
                        out=dst.ap, in_=ps.h[0:64, :], func=AF.Copy),
                        reads=[ps], writes=[dst])

                def dt_front(m):
                    ps = pool6[m % 6]
                    pr.op("tensor", lambda h, ps=ps, l=l, m=m: h.matmul(
                        out=ps.ap, lhsT=wdt_b[l].h[:, 128 * m:128 * (m + 1)],
                        rhs=dtr_bf.ap, start=True, stop=True),
                        reads=[wdt_b[l], dtr_bf], writes=[ps])
                    s1 = sp1[m % 2]; s2 = sp2[m % 2]
                    pr.op("scalar", lambda h, ps=ps, s1=s1, l=l, m=m: h.activation(
                        out=s1.ap, in_=ps.ap, func=AF.Abs, bias=dtb[l].h[:, m:m + 1]),
                        reads=[ps, dtb[l]], writes=[s1])
                    pr.op("scalar", lambda h, s1=s1: h.activation(
                        out=s1.ap, in_=s1.ap, func=AF.Exp, scale=-1.0),
                        reads=[s1], writes=[s1])
                    pr.op("scalar", lambda h, s1=s1: h.activation(
                        out=s1.ap, in_=s1.ap, func=AF.Ln, bias=1.0),
                        reads=[s1], writes=[s1])
                    pr.op("scalar", lambda h, ps=ps, s2=s2, l=l, m=m: h.activation(
                        out=s2.ap, in_=ps.ap, func=AF.Relu, bias=dtb[l].h[:, m:m + 1]),
                        reads=[ps, dtb[l]], writes=[s2])
                def dt_back(m):
                    s1 = sp1[m % 2]; s2 = sp2[m % 2]
                    pr.op("vector", lambda h, s1=s1, s2=s2, m=m: h.tensor_add(
                        out=dt_bf[m].ap, in0=s1.ap, in1=s2.ap),
                        reads=[s1, s2], writes=[dt_bf[m]])
                    pr.op("vector", lambda h, m=m: h.tensor_mul(
                        out=du_bf[m].ap, in0=dt_bf[m].ap, in1=u_bf[m].ap),
                        reads=[dt_bf[m], u_bf[m]], writes=[du_bf[m]])
                for m in range(KE + 2):
                    if m >= 2:
                        dt_back(m - 2)
                    if m < KE:
                        dt_front(m)

                for half in range(2):
                    ms = list(range(8 * half, 8 * half + 8))
                    for mi, m in enumerate(ms):
                        pr.op("vector", lambda h, mi=mi, m=m, l=l: h.tensor_scalar_mul(
                            out=yac[mi].ap, in0=u_bf[m].ap,
                            scalar1=dsk[l].h[:, m:m + 1]),
                            reads=[u_bf[m], dsk[l]], writes=[yac[mi]])
                    items = [(s2, mi) for s2 in range(0, S, 2) for mi in range(8)]
                    NI = len(items)
                    bix = lambda s2: 2 * ((s2 // 2) % 2)

                    def emit_bcast(s2):
                        bi = bix(s2)
                        for ds in range(2):
                            s = s2 + ds
                            sl = sel[ds]
                            pr.op("sync", lambda h, sl=sl, s=s: h.dma_start(
                                out=sl.ap, in_=sel_all[s]),
                                reads=[selab], writes=[sl], dma=True)
                            bb = pbc[ds]; cc = pcc[ds]
                            pr.op("tensor", lambda h, bb=bb, sl=sl: h.matmul(
                                out=bb.ap, lhsT=sl.ap, rhs=bmat_bf.ap,
                                start=True, stop=True),
                                reads=[sl, bmat_bf], writes=[bb])
                            pr.op("tensor", lambda h, cc=cc, sl=sl: h.matmul(
                                out=cc.ap, lhsT=sl.ap, rhs=cmat_bf.ap,
                                start=True, stop=True),
                                reads=[sl, cmat_bf], writes=[cc])
                            pr.op("scalar", lambda h, t=bbs4[bi + ds], bb=bb:
                                  h.activation(out=t.ap, in_=bb.ap, func=AF.Copy),
                                  reads=[bb], writes=[bbs4[bi + ds]])
                            pr.op("scalar", lambda h, t=ccs4[bi + ds], cc=cc:
                                  h.activation(out=t.ap, in_=cc.ap, func=AF.Copy),
                                  reads=[cc], writes=[ccs4[bi + ds]])

                    emit_bcast(0)
                    G3 = 2
                    NSG = NI // G3
                    def sg_items(j):
                        return [(k, items[k]) for k in
                                range(j * G3, min(NI, (j + 1) * G3)) if k < NI]
                    for j in range(NSG + 2):
                        for (k, (s2, mi)) in sg_items(j):
                            m = ms[mi]
                            if mi == 0 and s2 + 2 < S:
                                emit_bcast(s2 + 2)
                            for ds in range(2):
                                s = s2 + ds
                                da = da_t[(k * 2 + ds) % 8]
                                pr.op("scalar", lambda h, da=da, l=l, m=m, s=s:
                                      h.activation(
                                          out=da.ap, in_=dt_bf[m].ap, func=AF.Exp,
                                          scale=aneg[l].h[:, m * S + s:m * S + s + 1]),
                                      reads=[dt_bf[m], aneg[l]], writes=[da])
                        for (k, (s2, mi)) in sg_items(j):
                            m = ms[mi]
                            for ds in range(2):
                                bt = b_t[(k * 2 + ds) % 8]
                                bbuf = bbs4[bix(s2) + ds]
                                pr.op("vector", lambda h, bt=bt, m=m, bbuf=bbuf:
                                      h.tensor_mul(out=bt.ap, in0=du_bf[m].ap,
                                                   in1=bbuf.ap),
                                      reads=[du_bf[m], bbuf], writes=[bt])
                        if 2 <= j:
                            for (k, (s2, mi)) in sg_items(j - 2):
                                for ds in range(2):
                                    ht = h_t[(k * 2 + ds) % 8]
                                    pd = pd_t[(k * 2 + ds) % 4]
                                    cbuf = ccs4[bix(s2) + ds]
                                    pr.op("vector", lambda h, pd=pd, ht=ht, cbuf=cbuf:
                                          h.tensor_mul(out=pd.ap, in0=ht.ap,
                                                       in1=cbuf.ap),
                                          reads=[ht, cbuf], writes=[pd])
                            for (k, (s2, mi)) in sg_items(j - 2):
                                dst = pdh_t[mi] if (s2 % 4) == 0 else ppd_t[k % 2]
                                pa = pd_t[(k * 2) % 4]; pb = pd_t[(k * 2 + 1) % 4]
                                pr.op("vector", lambda h, dst=dst, pa=pa, pb=pb:
                                      h.tensor_add(out=dst.h[:, 0:TC], in0=pa.ap,
                                                   in1=pb.ap),
                                      reads=[pa, pb], writes=[dst])
                            for (k, (s2, mi)) in sg_items(j - 2):
                                if (s2 % 4) == 0:
                                    continue
                                pp = ppd_t[k % 2]
                                ph = pdh_t[mi]
                                pr.op("vector", lambda h, pp=pp, ph=ph: h.tensor_add(
                                    out=pp.ap, in0=pp.ap, in1=ph.h[:, 0:TC]),
                                    reads=[ph, pp], writes=[pp])
                            for (k, (s2, mi)) in sg_items(j - 2):
                                if (s2 % 4) == 0:
                                    continue
                                pp = ppd_t[k % 2]
                                ya = yac[mi]
                                pr.op("vector", lambda h, ya=ya, pp=pp: h.tensor_add(
                                    out=ya.ap, in0=ya.ap, in1=pp.ap),
                                    reads=[pp, ya], writes=[ya])
                            for (k, (s2, mi)) in sg_items(j - 2):
                                m = ms[mi]
                                for ds in range(2):
                                    ht = h_t[(k * 2 + ds) % 8]
                                    s = s2 + ds
                                    hc = hcol[l][m][s]
                                    col = m * S + s
                                    pr.op("scalar", lambda h, ht=ht, hc=hc, col=col:
                                          h.activation(out=hc.h[:, col:col + 1],
                                                       in_=ht.h[:, TC - 1:TC],
                                                       func=AF.Copy),
                                          reads=[ht], writes=[hc])
                        if 1 <= j:
                            for (k, (s2, mi)) in sg_items(j - 1):
                                m = ms[mi]
                                for ds in range(2):
                                    s = s2 + ds
                                    da = da_t[(k * 2 + ds) % 8]
                                    bt = b_t[(k * 2 + ds) % 8]
                                    ht = h_t[(k * 2 + ds) % 8]
                                    hc = hcol[l][m][s]
                                    col = m * S + s
                                    pr.op("vector",
                                          lambda h, ht=ht, da=da, bt=bt, hc=hc,
                                          col=col:
                                          h.tensor_tensor_scan(
                                              out=ht.ap, data0=da.ap, data1=bt.ap,
                                              initial=hc.h[:, col:col + 1],
                                              op0=ALU.mult, op1=ALU.add),
                                          reads=[da, bt, hc], writes=[ht])
                    def z_front(mi, m):
                        wt = wpool[m % 2]
                        pr.op("sync", lambda h, l=l, m=m, wt=wt: h.dma_start(
                            out=wt.h[:, 0:KD * 128],
                            in_=sc_in[l, KE + m].rearrange("p a b -> p (a b)")),
                            reads=[scb["sc_in"]], writes=[wt], dma=True)
                        ps = pool6[m % 6]
                        mm_acc(ps, lambda ps=ps: ps.ap,
                               [((lambda wt=wt, c=c: wt.h[:, 128 * c:128 * (c + 1)]),
                                 xin_bf[c], [wt]) for c in range(KD)])
                        zs = zs_t[mi % 2]
                        pr.op("scalar", lambda h, zs=zs, ps=ps: h.activation(
                            out=zs.ap, in_=ps.ap, func=AF.Silu),
                            reads=[ps], writes=[zs])
                    def z_back(mi, m):
                        ya = yac[mi]
                        zs = zs_t[mi % 2]
                        pr.op("vector", lambda h, ya=ya, zs=zs, m=m: h.tensor_mul(
                            out=du_bf[m].ap, in0=ya.ap, in1=zs.ap),
                            reads=[ya, zs], writes=[du_bf[m]])
                    for mi in range(10):
                        if mi >= 2:
                            z_back(mi - 2, ms[mi - 2])
                        if mi < 8:
                            z_front(mi, ms[mi])

                def op_front(k):
                    wt = wpool[k % 2]
                    pr.op("sync", lambda h, l=l, k=k, wt=wt: h.dma_start(
                        out=wt.h[:, 0:KE * 128],
                        in_=sc_out[l, k].rearrange("p a b -> p (a b)")),
                        reads=[scb["sc_out"]], writes=[wt], dma=True)
                    ps = pool6[k % 6]
                    mm_acc(ps, lambda ps=ps: ps.ap,
                           [((lambda wt=wt, c=c: wt.h[:, 128 * c:128 * (c + 1)]),
                             du_bf[c], [wt]) for c in range(KE)])
                def op_back(k):
                    ps = pool6[k % 6]
                    if l == 0:
                        pr.op("scalar", lambda h, ps=ps, k=k: h.activation(
                            out=xb[k].ap, in_=ps.ap, func=AF.Copy),
                            reads=[ps], writes=[xb[k]])
                    else:
                        pr.op("vector", lambda h, ps=ps, k=k: h.tensor_add(
                            out=xf[k].ap, in0=ps.ap, in1=xf[k].ap),
                            reads=[ps, xf[k]], writes=[xf[k]])
                for k in range(KD + 2):
                    if k < KD:
                        op_front(k)
                    if k >= 2:
                        op_back(k - 2)
            def layernorm(src_f, dst_f, dst_b):
                for k in range(KD):
                    pr.op("tensor", lambda h, k=k: h.matmul(
                        out=ps1.ap, lhsT=ones_col.ap, rhs=src_f[k].ap,
                        start=(k == 0), stop=(k == KD - 1)),
                        reads=[ones_col, src_f[k]], writes=[ps1])
                for k in range(KD):
                    sq = sq_t[k % 2]
                    pr.op("scalar", lambda h, k=k, sq=sq: h.activation(
                        out=sq.ap, in_=src_f[k].ap, func=AF.Square),
                        reads=[src_f[k]], writes=[sq])
                    pr.op("tensor", lambda h, k=k, sq=sq: h.matmul(
                        out=ps2.ap, lhsT=ones_col.ap, rhs=sq.ap,
                        start=(k == 0), stop=(k == KD - 1)),
                        reads=[ones_col, sq], writes=[ps2])
                pr.op("vector", lambda h: h.tensor_scalar_mul(
                    out=row_a.ap, in0=ps1.ap, scalar1=1.0 / D),
                    reads=[ps1], writes=[row_a])
                pr.op("vector", lambda h: h.tensor_mul(
                    out=row_b.ap, in0=row_a.ap, in1=row_a.ap),
                    reads=[row_a], writes=[row_b])
                pr.op("vector", lambda h: h.scalar_tensor_tensor(
                    out=row_b.ap, in0=ps2.ap, scalar=1.0 / D, in1=row_b.ap,
                    op0=ALU.mult, op1=ALU.subtract),
                    reads=[ps2, row_b], writes=[row_b])
                pr.op("scalar", lambda h: h.activation(
                    out=row_b.ap, in_=row_b.ap, func=AF.Ln, bias=epsrow.ap),
                    reads=[row_b, epsrow], writes=[row_b])
                pr.op("scalar", lambda h: h.activation(
                    out=row_b.ap, in_=row_b.ap, func=AF.Exp, scale=-0.5),
                    reads=[row_b], writes=[row_b])
                pr.op("vector", lambda h: h.tensor_mul(
                    out=row_c.ap, in0=row_a.ap, in1=row_b.ap),
                    reads=[row_a, row_b], writes=[row_c])
                pr.op("tensor", lambda h: h.matmul(
                    out=pbc[0].ap, lhsT=ones_row.ap, rhs=row_b.ap,
                    start=True, stop=True),
                    reads=[ones_row, row_b], writes=[pbc[0]])
                pr.op("tensor", lambda h: h.matmul(
                    out=pcc[0].ap, lhsT=ones_row.ap, rhs=row_c.ap,
                    start=True, stop=True),
                    reads=[ones_row, row_c], writes=[pcc[0]])
                for k in range(KD):
                    pr.op("vector", lambda h, k=k: h.tensor_mul(
                        out=dst_f[k].ap, in0=src_f[k].ap, in1=pbc[0].ap),
                        reads=[src_f[k], pbc[0]], writes=[dst_f[k]])
                    pr.op("vector", lambda h, k=k: h.tensor_sub(
                        out=dst_f[k].ap, in0=dst_f[k].ap, in1=pcc[0].ap),
                        reads=[dst_f[k], pcc[0]], writes=[dst_f[k]])
                    if dst_b is not None:
                        pr.op("scalar", lambda h, k=k: h.activation(
                            out=dst_b[k].ap, in_=dst_f[k].ap, func=AF.Copy),
                            reads=[dst_f[k]], writes=[dst_b[k]])

            layernorm(xf, xf, xb)

            hbt = u_bf + dt_bf[:KH - KE]
            def ffn_front(m):
                wt = wpool[m % 2]
                pr.op("sync", lambda h, m=m, wt=wt: h.dma_start(
                    out=wt.h[:, 0:2 * KD * 128],
                    in_=sc_gu[m].rearrange("p a b -> p (a b)")),
                    reads=[scb["sc_gu"]], writes=[wt], dma=True)
                psg = pool6[(2 * m) % 6]; psu = pool6[(2 * m + 1) % 6]
                mm_acc(psg, lambda psg=psg: psg.ap,
                       [((lambda wt=wt, c=c: wt.h[:, 128 * c:128 * (c + 1)]),
                         xb[c], [wt]) for c in range(KD)])
                mm_acc(psu, lambda psu=psu: psu.ap,
                       [((lambda wt=wt, c=c: wt.h[:, 128 * (KD + c):128 * (KD + c + 1)]),
                         xb[c], [wt]) for c in range(KD)])
                gs = zs_t[m % 2]
                pr.op("scalar", lambda h, gs=gs, psg=psg: h.activation(
                    out=gs.ap, in_=psg.ap, func=AF.Silu, scale=1.0 / 127.0),
                    reads=[psg], writes=[gs])
            def ffn_back(m):
                psu = pool6[(2 * m + 1) % 6]
                gs = zs_t[m % 2]
                pr.op("vector", lambda h, m=m, gs=gs, psu=psu: h.scalar_tensor_tensor(
                    out=hbt[m].ap, in0=psu.ap, scalar=1.0 / 127.0, in1=gs.ap,
                    op0=ALU.mult, op1=ALU.mult),
                    reads=[psu, gs], writes=[hbt[m]])
            for m in range(KH + 1):
                if m >= 1:
                    ffn_back(m - 1)
                if m < KH:
                    ffn_front(m)
            for k in range(KD):
                wta, wtb = wpool[0], wpool[1]
                pr.op("sync", lambda h, k=k, wta=wta: h.dma_start(
                    out=wta.h[:, 0:12 * 128],
                    in_=sc_d[k, :, 0:12, :].rearrange("p a b -> p (a b)")),
                    reads=[scb["sc_d"]], writes=[wta], dma=True)
                pr.op("sync", lambda h, k=k, wtb=wtb: h.dma_start(
                    out=wtb.h[:, 0:12 * 128],
                    in_=sc_d[k, :, 12:24, :].rearrange("p a b -> p (a b)")),
                    reads=[scb["sc_d"]], writes=[wtb], dma=True)
                ps = pool6[k % 6]
                mm_acc(ps, lambda ps=ps: ps.ap,
                       [((lambda wt=(wta if c < 12 else wtb), c=c:
                          wt.h[:, 128 * (c % 12):128 * (c % 12 + 1)]),
                         hbt[c], [wta if c < 12 else wtb]) for c in range(KH)])
                pr.op("vector", lambda h, ps=ps, k=k: h.scalar_tensor_tensor(
                    out=xf[k].ap, in0=ps.ap, scalar=1.0 / 127.0, in1=xf[k].ap,
                    op0=ALU.mult, op1=ALU.add),
                    reads=[ps, xf[k]], writes=[xf[k]])
            layernorm(xf, xf, None)

            for k in range(KD):
                pr.op("gpsimd", lambda h, k=k, t0=t0: h.dma_start(
                    out=outT[128 * k:128 * (k + 1), t0:t0 + TC], in_=xf[k].ap),
                    reads=[xf[k]], dma=True)

        pr.wait("gpsimd", pr.barrier_tok("dma_gpsimd"))
        pr.emit(nc, es)
    return nc



def _prep_inputs(L0_attn_x, L0_attn_in_proj_w, L0_attn_conv_w, L0_attn_conv_b,
                 L0_attn_x_proj_w, L0_attn_dt_proj_w, L0_attn_dt_proj_b,
                 L0_attn_A_log, L0_attn_D_skip, L0_attn_out_proj_w,
                 L0_ffn_w_gate, L0_ffn_w_up, L0_ffn_w_down):
    C = lambda a: np.ascontiguousarray(a)
    A = lambda a: np.asarray(a)
    shared = {
        "w_in": C(np.transpose(A(L0_attn_in_proj_w), (0, 2, 1))),
        "w_xp": C(np.transpose(A(L0_attn_x_proj_w), (0, 2, 1))),
        "w_dt": C(np.transpose(A(L0_attn_dt_proj_w), (0, 2, 1))),
        "w_out": C(np.transpose(A(L0_attn_out_proj_w), (0, 2, 1))),
        "cw_r": C(A(L0_attn_conv_w).reshape(NL, KE, 128, 4)
                  .transpose(0, 2, 1, 3).reshape(NL, 128, KE * 4)),
        "cb_r": C(A(L0_attn_conv_b).reshape(NL, KE, 128).transpose(0, 2, 1)),
        "dtb_r": C(A(L0_attn_dt_proj_b).reshape(NL, KE, 128).transpose(0, 2, 1)),
        "dsk_r": C(A(L0_attn_D_skip).reshape(NL, KE, 128).transpose(0, 2, 1)),
        "alog_r": C(A(L0_attn_A_log).reshape(NL, KE, 128, S)
                    .transpose(0, 2, 1, 3).reshape(NL, 128, KE * S)),
        "sel_all": np.repeat(np.eye(S, dtype=np.float32)[:, :, None]
                             .astype(ml_dtypes.bfloat16), 128, axis=2),
        "wg": C(A(L0_ffn_w_gate).T),
        "wu": C(A(L0_ffn_w_up).T),
        "wd": C(A(L0_ffn_w_down).T),
    }
    x = A(L0_attn_x)
    per_core = [dict(shared, xT=C(x[i].T)) for i in range(x.shape[0])]
    return per_core


def kernel(**inputs):
    per_core = _prep_inputs(**inputs)
    nc = build_core()
    res = run_bass_kernel_spmd(nc, per_core, core_ids=list(range(8)))
    outs = [np.ascontiguousarray(r["outT"].T) for r in res.results]
    return np.stack(outs, 0).astype(np.float32)


# version 26 (final)
# speedup vs baseline: 1.1340x; 1.0109x over previous
import contextlib
import numpy as np
import ml_dtypes
import concourse.bass as bass
import concourse.mybir as mybir
from concourse.bass_utils import run_bass_kernel_spmd

F32 = mybir.dt.float32
BF16 = mybir.dt.bfloat16
I32 = mybir.dt.int32
AF = mybir.ActivationFunctionType
ALU = mybir.AluOpType


class Tok:
    __slots__ = ("k", "v")
    def __init__(self, k, v):
        self.k = k
        self.v = v


class Buf:
    __slots__ = ("h", "w", "r", "name")
    def __init__(self, h, name=""):
        self.h = h
        self.name = name or getattr(h, "name", "")
        self.w = None
        self.r = []
    def __getitem__(self, idx):
        return self.h[idx]
    @property
    def ap(self):
        return self.h[:]


ENGINES = ("tensor", "vector", "scalar", "gpsimd", "sync")


class Prog:
    def __init__(self):
        self.items = {e: [] for e in ENGINES}
        self.counts = {}
        for e in ENGINES:
            self.counts[e] = 0
            self.counts["dma_" + e] = 0
        self.wm = {e: {} for e in ENGINES}
        self.pending = {"dma_" + e: [] for e in ENGINES}

    def _fence(self, key):
        plist = self.pending[key]
        if not plist:
            return
        total = self.counts[key]
        for t in plist:
            t.v = total
        plist.clear()
        eng = key[4:]
        self.items[eng].append(([(key, total)], None, None, 0))
        wm = self.wm[eng]
        if wm.get(key, 0) < total:
            wm[key] = total

    def op(self, eng, fn, reads=(), writes=(), dma=False):
        toks = []
        for b in reads:
            if b.w:
                toks.append(b.w)
        for b in writes:
            if b.w:
                toks.append(b.w)
            toks.extend(b.r)
        for t in toks:
            if t.v is None:
                self._fence(t.k)
        best = {}
        for t in toks:
            if t.v > best.get(t.k, -1):
                best[t.k] = t.v
        wm = self.wm[eng]
        waits = []
        for k in sorted(best):
            v = best[k]
            if wm.get(k, 0) < v:
                waits.append((k, v))
                wm[k] = v
        key = ("dma_" + eng) if dma else eng
        step = 16 if dma else 1
        self.counts[key] += step
        if dma:
            tok = Tok(key, None)
            self.pending[key].append(tok)
        else:
            tok = Tok(key, self.counts[key])
        self.items[eng].append((waits, fn, key, step))
        for b in writes:
            b.w = tok
            b.r = []
        for b in reads:
            b.r.append(tok)
        return tok

    def barrier_tok(self, key):
        if key.startswith("dma_"):
            self._fence(key)
        return Tok(key, self.counts[key])

    def wait(self, eng, tok):
        if tok.v is None:
            self._fence(tok.k)
        wm = self.wm[eng]
        if wm.get(tok.k, 0) >= tok.v:
            return
        wm[tok.k] = tok.v
        self.items[eng].append(([(tok.k, tok.v)], None, None, 0))

    def emit(self, nc, es):
        used = set(k for lst in self.items.values()
                   for (ws, _f, _k, _s) in lst for (k, _v) in ws)
        used |= set(k for k, c in self.counts.items() if c > 0)
        sems = {k: es.enter_context(nc.semaphore("sq_" + k)) for k in sorted(used)}
        block = es.enter_context(nc.Block())
        for e in ENGINES:
            lst = self.items[e]
            if not lst:
                continue
            def make(lst):
                def body(h):
                    for (waits, fn, key, step) in lst:
                        for (k, v) in waits:
                            h.wait_ge(sems[k], v)
                        if fn is not None:
                            fn(h).then_inc(sems[key], step)
                return body
            getattr(block, e)(make(lst))



D = 1024
E = 2048
S = 64
RK = 64
HID = 3072
NL = 2
KD = D // 128
KE = E // 128
KH = HID // 128
EPS = 1e-5


def build_core(L=4096, TC=512):
    NT = L // TC
    nc = bass.Bass("TRN2", target_bir_lowering=False)

    xT = nc.dram_tensor("xT", [D, L], F32, kind="ExternalInput").ap()
    outT = nc.dram_tensor("outT", [D, L], F32, kind="ExternalOutput").ap()
    w_in = nc.dram_tensor("w_in", [NL, D, 2 * E], F32, kind="ExternalInput").ap()
    w_xp = nc.dram_tensor("w_xp", [NL, E, 192], F32, kind="ExternalInput").ap()
    w_dt = nc.dram_tensor("w_dt", [NL, RK, E], F32, kind="ExternalInput").ap()
    w_out = nc.dram_tensor("w_out", [NL, E, D], F32, kind="ExternalInput").ap()
    cw_r = nc.dram_tensor("cw_r", [NL, 128, KE * 4], F32, kind="ExternalInput").ap()
    cb_r = nc.dram_tensor("cb_r", [NL, 128, KE], F32, kind="ExternalInput").ap()
    dtb_r = nc.dram_tensor("dtb_r", [NL, 128, KE], F32, kind="ExternalInput").ap()
    dsk_r = nc.dram_tensor("dsk_r", [NL, 128, KE], F32, kind="ExternalInput").ap()
    alog_r = nc.dram_tensor("alog_r", [NL, 128, KE * S], F32, kind="ExternalInput").ap()
    sel_all = nc.dram_tensor("sel_all", [S, S, 128], BF16, kind="ExternalInput").ap()
    wg = nc.dram_tensor("wg", [D, HID], I32, kind="ExternalInput").ap()
    wu = nc.dram_tensor("wu", [D, HID], I32, kind="ExternalInput").ap()
    wd = nc.dram_tensor("wd", [HID, D], I32, kind="ExternalInput").ap()

    sc_in = nc.dram_tensor("sc_in", [NL, 2 * KE, 128, KD, 128], BF16,
                           kind="Internal").ap()
    sc_out = nc.dram_tensor("sc_out", [NL, KD, 128, KE, 128], BF16,
                            kind="Internal").ap()
    sc_gu = nc.dram_tensor("sc_gu", [KH, 128, 2 * KD, 128], BF16,
                           kind="Internal").ap()
    sc_d = nc.dram_tensor("sc_d", [KD, 128, KH, 128], BF16, kind="Internal").ap()

    es = contextlib.ExitStack()
    pr = Prog()
    with es:
        def T(n, shp, dt=F32):
            return Buf(es.enter_context(nc.sbuf_tensor(n, shp, dt)), n)
        def PT(n, shp):
            return Buf(es.enter_context(nc.psum_tensor(n, shp, F32)), n)

        ones_bf = T("ones_bf", [1, 128], BF16)
        ones_col = T("ones_col", [128, 1], F32)
        ones_row = T("ones_row", [1, 128], F32)
        epsrow = T("epsrow", [1, 1], F32)
        pr.op("vector", lambda h: h.memset(ones_bf.ap, 1.0), writes=[ones_bf])
        pr.op("vector", lambda h: h.memset(ones_col.ap, 1.0), writes=[ones_col])
        pr.op("vector", lambda h: h.memset(ones_row.ap, 1.0), writes=[ones_row])
        pr.op("vector", lambda h: h.memset(epsrow.ap, EPS), writes=[epsrow])

        cw = [T(f"cw{l}", [128, KE * 4]) for l in range(NL)]
        cb = [T(f"cb{l}", [128, KE]) for l in range(NL)]
        dtb = [T(f"dtb{l}", [128, KE]) for l in range(NL)]
        dsk = [T(f"dsk{l}", [128, KE]) for l in range(NL)]
        aneg = [T(f"aneg{l}", [128, KE * S]) for l in range(NL)]
        wxp_b = [T(f"wxp{l}", [128, KE * 192], BF16) for l in range(NL)]
        wdt_b = [T(f"wdtb{l}", [RK, E], BF16) for l in range(NL)]

        for l in range(NL):
            for (dst, src) in [(cw[l], cw_r[l]), (cb[l], cb_r[l]),
                               (dtb[l], dtb_r[l]), (dsk[l], dsk_r[l]),
                               (aneg[l], alog_r[l])]:
                pr.op("sync", lambda h, dst=dst, src=src: h.dma_start(
                    out=dst.ap, in_=src), writes=[dst], dma=True)
            pr.op("scalar", lambda h, l=l: h.activation(
                out=aneg[l].ap, in_=aneg[l].ap, func=AF.Exp),
                reads=[aneg[l]], writes=[aneg[l]])
            pr.op("vector", lambda h, l=l: h.tensor_scalar_mul(
                out=aneg[l].ap, in0=aneg[l].ap, scalar1=-1.0),
                reads=[aneg[l]], writes=[aneg[l]])

        stg_f = T("stg_f", [128, 256], F32)
        stg_i = Buf(es.enter_context(nc.sbuf_tensor("stg_i", [128, 256], I32)), "stg_i")
        stg_b = T("stg_b", [128, 256], BF16)
        scb = {"sc_in": Buf(None, "sc_in"), "sc_out": Buf(None, "sc_out"),
               "sc_gu": Buf(None, "sc_gu"), "sc_d": Buf(None, "sc_d")}

        for l in range(NL):
            for c in range(KE):
                pr.op("sync", lambda h, l=l, c=c: h.dma_start(
                    out=stg_f.h[:, 0:192], in_=w_xp[l, 128 * c:128 * (c + 1), :]),
                    writes=[stg_f], dma=True)
                pr.op("gpsimd", lambda h, l=l, c=c: h.tensor_copy(
                    out=wxp_b[l].h[:, 192 * c:192 * (c + 1)], in_=stg_f.h[:, 0:192]),
                    reads=[stg_f], writes=[wxp_b[l]])
            for g in range(8):
                pr.op("sync", lambda h, l=l, g=g: h.dma_start(
                    out=stg_f.h[0:RK, :], in_=w_dt[l, :, 256 * g:256 * (g + 1)]),
                    writes=[stg_f], dma=True)
                pr.op("gpsimd", lambda h, l=l, g=g: h.tensor_copy(
                    out=wdt_b[l].h[:, 256 * g:256 * (g + 1)], in_=stg_f.h[0:RK, :]),
                    reads=[stg_f], writes=[wdt_b[l]])

        def run_prepass(get_rows, get_dst, n_k, n_m, is_int):
            GM = 2
            stg_in = stg_i if is_int else stg_f
            for c in range(n_k):
                rows = get_rows(c)
                for g0 in range(0, n_m, GM):
                    gm = min(GM, n_m - g0)
                    pr.op("sync", lambda h, rows=rows, g0=g0, gm=gm, stg_in=stg_in:
                          h.dma_start(out=stg_in.h[:, 0:gm * 128],
                                      in_=rows[:, 128 * g0:128 * (g0 + gm)]),
                          writes=[stg_in], dma=True)
                    pr.op("gpsimd", lambda h, gm=gm, stg_in=stg_in: h.tensor_copy(
                        out=stg_b.h[:, 0:gm * 128], in_=stg_in.h[:, 0:gm * 128]),
                        reads=[stg_in], writes=[stg_b])
                    dst = get_dst(c, g0, gm)
                    pr.op("gpsimd", lambda h, dst=dst, gm=gm: h.dma_start(
                        out=dst.rearrange("a p b -> p a b"),
                        in_=stg_b.h[:, 0:gm * 128]
                        .rearrange("p (a b) -> p a b", a=gm)),
                        reads=[stg_b], writes=[scb_cur[0]], dma=True)

        for l in range(NL):
            scb_cur = [scb["sc_in"]]
            run_prepass(lambda c, l=l: w_in[l, 128 * c:128 * (c + 1), :],
                        lambda c, g0, gm, l=l: sc_in[l, g0:g0 + gm, :, c, :],
                        KD, 2 * KE, False)
            scb_cur = [scb["sc_out"]]
            run_prepass(lambda c, l=l: w_out[l, 128 * c:128 * (c + 1), :],
                        lambda c, g0, gm, l=l: sc_out[l, g0:g0 + gm, :, c, :],
                        KE, KD, False)
        scb_cur = [scb["sc_gu"]]
        run_prepass(lambda c: wg[128 * c:128 * (c + 1), :],
                    lambda c, g0, gm: sc_gu[g0:g0 + gm, :, c, :], KD, KH, True)
        run_prepass(lambda c: wu[128 * c:128 * (c + 1), :],
                    lambda c, g0, gm: sc_gu[g0:g0 + gm, :, KD + c, :], KD, KH, True)
        scb_cur = [scb["sc_d"]]
        run_prepass(lambda c: wd[128 * c:128 * (c + 1), :],
                    lambda c, g0, gm: sc_d[g0:g0 + gm, :, c, :], KH, KD, True)

        xf = [T(f"xf{k}", [128, TC]) for k in range(KD)]
        xb = [T(f"xb{k}", [128, TC], BF16) for k in range(KD)]
        u_ext = [T(f"uext{i}", [128, TC + 3], BF16) for i in range(4)]
        u_bf = [T(f"ubf{m}", [128, TC], BF16) for m in range(KE)]
        dt_bf = [T(f"dtbf{m}", [128, TC], BF16) for m in range(KE)]
        du_bf = [T(f"dubf{m}", [128, TC], BF16) for m in range(KE)]
        conv_acc = [T(f"cacc{i}", [128, TC]) for i in range(2)]
        sp1 = [T(f"sp1_{i}", [128, TC], BF16) for i in range(2)]
        sp2 = [T(f"sp2_{i}", [128, TC], BF16) for i in range(2)]
        dtr_bf = T("dtr_bf", [RK, TC], BF16)
        bmat_bf = T("bmat_bf", [S, TC], BF16)
        cmat_bf = T("cmat_bf", [S, TC], BF16)
        da_t = [T(f"da{i}", [128, TC]) for i in range(8)]
        b_t = [T(f"b{i}", [128, TC], BF16) for i in range(8)]
        h_t = [T(f"h{i}", [128, TC], BF16) for i in range(8)]
        pd_t = [T(f"pd{i}", [128, TC], BF16) for i in range(4)]
        ppd_t = [T(f"ppd{i}", [128, TC], BF16) for i in range(2)]
        pdh_t = [sp1[0], sp1[1], sp2[0], sp2[1]] + u_ext
        bbs4 = [T(f"bbs{i}", [128, TC], BF16) for i in range(4)]
        ccs4 = [T(f"ccs{i}", [128, TC], BF16) for i in range(4)]
        yac = [T(f"yac{m}", [128, TC]) for m in range(8)]
        zs_t = [T(f"zs{i}", [128, TC]) for i in range(2)]
        sq_t = [T("sq_0", [128, TC])] * 2
        row_a = T("row_a", [1, TC])
        row_b = T("row_b", [1, TC])
        row_c = T("row_c", [1, TC])

        cvtail = [[T(f"cvt{l}_{m}", [128, 3], BF16) for m in range(KE)]
                  for l in range(NL)]
        hcar = [T(f"hcar{l}", [128, KE * S], BF16) for l in range(NL)]
        for l in range(NL):
            for m in range(KE):
                pr.op("vector", lambda h, l=l, m=m: h.memset(cvtail[l][m].ap, 0.0),
                      writes=[cvtail[l][m]])
            pr.op("vector", lambda h, l=l: h.memset(hcar[l].ap, 0.0),
                  writes=[hcar[l]])
        hcol = [[[Buf(hcar[l].h, f"hc{l}_{m}_{s}") for s in range(S)]
                 for m in range(KE)] for l in range(NL)]
        for l in range(NL):
            for m in range(KE):
                for s in range(S):
                    hcol[l][m][s].w = hcar[l].w

        wpool = [T(f"wpool{i}", [128, KE * 128], BF16) for i in range(2)]

        sel = [T(f"sel{i}", [S, 128], BF16) for i in range(2)]
        selab = Buf(None, "sel_all")

        pm = [PT(f"pm{i}", [128, TC]) for i in range(2)]
        pbc = [PT(f"pbc{i}", [128, TC]) for i in range(2)]
        pcc = [PT(f"pcc{i}", [128, TC]) for i in range(2)]
        pool6 = pm + pbc + pcc
        ps1 = PT("ps1", [1, TC])
        ps2 = PT("ps2", [1, TC])

        def mm_acc(dst_ps, out_ap_fn, pairs):
            n = len(pairs)
            for i, (lap, rhs, lbs) in enumerate(pairs):
                pr.op("tensor", lambda h, lap=lap, rhs=rhs, i=i:
                      h.matmul(out=out_ap_fn(), lhsT=lap(), rhs=rhs.ap,
                               start=(i == 0), stop=(i == n - 1)),
                      reads=[rhs] + lbs, writes=[dst_ps])

        for tc in range(NT):
            t0 = tc * TC
            for k in range(KD):
                pr.op("sync", lambda h, k=k, t0=t0: h.dma_start(
                    out=xf[k].ap, in_=xT[128 * k:128 * (k + 1), t0:t0 + TC]),
                    writes=[xf[k]], dma=True)
                pr.op("scalar", lambda h, k=k: h.activation(
                    out=xb[k].ap, in_=xf[k].ap, func=AF.Copy),
                    reads=[xf[k]], writes=[xb[k]])

            for l in range(NL):
                xin_bf = xb
                def inproj_front(m):
                    wt = wpool[m % 2]
                    pr.op("sync", lambda h, l=l, m=m, wt=wt: h.dma_start(
                        out=wt.h[:, 0:KD * 128],
                        in_=sc_in[l, m].rearrange("p a b -> p (a b)")),
                        reads=[scb["sc_in"]], writes=[wt], dma=True)
                    ps = pool6[m % 6]
                    mm_acc(ps, lambda ps=ps: ps.ap,
                           [((lambda wt=wt, c=c: wt.h[:, 128 * c:128 * (c + 1)]),
                             xin_bf[c], [wt]) for c in range(KD)])
                    ue = u_ext[m % 4]
                    pr.op("scalar", lambda h, ue=ue, ps=ps: h.activation(
                        out=ue.h[:, 3:TC + 3], in_=ps.ap, func=AF.Copy),
                        reads=[ps], writes=[ue])
                def inproj_back(m):
                    ue = u_ext[m % 4]
                    pr.op("vector", lambda h, ue=ue, l=l, m=m: h.tensor_copy(
                        out=ue.h[:, 0:3], in_=cvtail[l][m].ap),
                        reads=[cvtail[l][m]], writes=[ue])
                    pr.op("vector", lambda h, ue=ue, l=l, m=m: h.tensor_copy(
                        out=cvtail[l][m].ap, in_=ue.h[:, TC:TC + 3]),
                        reads=[ue], writes=[cvtail[l][m]])
                    ca = conv_acc[m % 2]
                    pr.op("vector", lambda h, ca=ca, ue=ue, l=l, m=m:
                          h.tensor_scalar_mul(
                              out=ca.ap, in0=ue.h[:, 0:TC],
                              scalar1=cw[l].h[:, 4 * m:4 * m + 1]),
                          reads=[ue, cw[l]], writes=[ca])
                    for k in range(1, 4):
                        pr.op("vector", lambda h, ca=ca, ue=ue, l=l, m=m, k=k:
                              h.scalar_tensor_tensor(
                                  out=ca.ap, in0=ue.h[:, k:k + TC],
                                  scalar=cw[l].h[:, 4 * m + k:4 * m + k + 1],
                                  in1=ca.ap, op0=ALU.mult, op1=ALU.add),
                              reads=[ue, cw[l], ca], writes=[ca])
                    pr.op("scalar", lambda h, ca=ca, l=l, m=m: h.activation(
                        out=u_bf[m].ap, in_=ca.ap, func=AF.Silu,
                        bias=cb[l].h[:, m:m + 1]),
                        reads=[ca, cb[l]], writes=[u_bf[m]])
                for m in range(KE + 2):
                    if m < KE:
                        inproj_front(m)
                    if m >= 2:
                        inproj_back(m - 2)

                for gi, (dst, off) in enumerate([(dtr_bf, 0), (bmat_bf, 64),
                                                 (cmat_bf, 128)]):
                    ps = pool6[gi % 6]
                    mm_acc(ps, lambda ps=ps: ps.h[0:64, :],
                           [((lambda c=c, l=l, off=off:
                              wxp_b[l].h[:, 192 * c + off:192 * c + off + 64]),
                             u_bf[c], [wxp_b[l]]) for c in range(KE)])
                    pr.op("scalar", lambda h, ps=ps, dst=dst: h.activation(
                        out=dst.ap, in_=ps.h[0:64, :], func=AF.Copy),
                        reads=[ps], writes=[dst])

                def dt_front(m):
                    ps = pool6[m % 6]
                    pr.op("tensor", lambda h, ps=ps, l=l, m=m: h.matmul(
                        out=ps.ap, lhsT=wdt_b[l].h[:, 128 * m:128 * (m + 1)],
                        rhs=dtr_bf.ap, start=True, stop=True),
                        reads=[wdt_b[l], dtr_bf], writes=[ps])
                    s1 = sp1[m % 2]; s2 = sp2[m % 2]
                    pr.op("scalar", lambda h, ps=ps, s1=s1, l=l, m=m: h.activation(
                        out=s1.ap, in_=ps.ap, func=AF.Abs, bias=dtb[l].h[:, m:m + 1]),
                        reads=[ps, dtb[l]], writes=[s1])
                    pr.op("scalar", lambda h, s1=s1: h.activation(
                        out=s1.ap, in_=s1.ap, func=AF.Exp, scale=-1.0),
                        reads=[s1], writes=[s1])
                    pr.op("scalar", lambda h, s1=s1: h.activation(
                        out=s1.ap, in_=s1.ap, func=AF.Ln, bias=1.0),
                        reads=[s1], writes=[s1])
                    pr.op("scalar", lambda h, ps=ps, s2=s2, l=l, m=m: h.activation(
                        out=s2.ap, in_=ps.ap, func=AF.Relu, bias=dtb[l].h[:, m:m + 1]),
                        reads=[ps, dtb[l]], writes=[s2])
                def dt_back(m):
                    s1 = sp1[m % 2]; s2 = sp2[m % 2]
                    pr.op("vector", lambda h, s1=s1, s2=s2, m=m: h.tensor_add(
                        out=dt_bf[m].ap, in0=s1.ap, in1=s2.ap),
                        reads=[s1, s2], writes=[dt_bf[m]])
                    pr.op("vector", lambda h, m=m: h.tensor_mul(
                        out=du_bf[m].ap, in0=dt_bf[m].ap, in1=u_bf[m].ap),
                        reads=[dt_bf[m], u_bf[m]], writes=[du_bf[m]])
                for m in range(KE + 2):
                    if m >= 2:
                        dt_back(m - 2)
                    if m < KE:
                        dt_front(m)

                for half in range(2):
                    ms = list(range(8 * half, 8 * half + 8))
                    pdh2_t = [u_bf[m] for m in ms]
                    def yac_preinit(mi):
                        m = ms[mi]
                        pr.op("scalar", lambda h, mi=mi, m=m, l=l: h.activation(
                            out=yac[mi].ap, in_=u_bf[m].ap, func=AF.Copy,
                            scale=dsk[l].h[:, m:m + 1]),
                            reads=[u_bf[m], dsk[l]], writes=[yac[mi]])
                    items = [(s2, mi) for s2 in range(0, S, 2) for mi in range(8)]
                    NI = len(items)
                    bix = lambda s2: 2 * ((s2 // 2) % 2)

                    def emit_bcast(s2):
                        bi = bix(s2)
                        for ds in range(2):
                            s = s2 + ds
                            sl = sel[ds]
                            pr.op("sync", lambda h, sl=sl, s=s: h.dma_start(
                                out=sl.ap, in_=sel_all[s]),
                                reads=[selab], writes=[sl], dma=True)
                            bb = pbc[ds]; cc = pcc[ds]
                            pr.op("tensor", lambda h, bb=bb, sl=sl: h.matmul(
                                out=bb.ap, lhsT=sl.ap, rhs=bmat_bf.ap,
                                start=True, stop=True),
                                reads=[sl, bmat_bf], writes=[bb])
                            pr.op("tensor", lambda h, cc=cc, sl=sl: h.matmul(
                                out=cc.ap, lhsT=sl.ap, rhs=cmat_bf.ap,
                                start=True, stop=True),
                                reads=[sl, cmat_bf], writes=[cc])
                            pr.op("scalar", lambda h, t=bbs4[bi + ds], bb=bb:
                                  h.activation(out=t.ap, in_=bb.ap, func=AF.Copy),
                                  reads=[bb], writes=[bbs4[bi + ds]])
                            pr.op("scalar", lambda h, t=ccs4[bi + ds], cc=cc:
                                  h.activation(out=t.ap, in_=cc.ap, func=AF.Copy),
                                  reads=[cc], writes=[ccs4[bi + ds]])

                    emit_bcast(0)
                    G3 = 2
                    NSG = NI // G3
                    def sg_items(j):
                        return [(k, items[k]) for k in
                                range(j * G3, min(NI, (j + 1) * G3)) if k < NI]
                    for j in range(NSG + 2):
                        if j < 8:
                            yac_preinit(j)
                        for (k, (s2, mi)) in sg_items(j):
                            m = ms[mi]
                            if mi == 0 and s2 + 2 < S:
                                emit_bcast(s2 + 2)
                            for ds in range(2):
                                s = s2 + ds
                                da = da_t[(k * 2 + ds) % 8]
                                pr.op("scalar", lambda h, da=da, l=l, m=m, s=s:
                                      h.activation(
                                          out=da.ap, in_=dt_bf[m].ap, func=AF.Exp,
                                          scale=aneg[l].h[:, m * S + s:m * S + s + 1]),
                                      reads=[dt_bf[m], aneg[l]], writes=[da])
                        for (k, (s2, mi)) in sg_items(j):
                            m = ms[mi]
                            for ds in range(2):
                                bt = b_t[(k * 2 + ds) % 8]
                                bbuf = bbs4[bix(s2) + ds]
                                pr.op("vector", lambda h, bt=bt, m=m, bbuf=bbuf:
                                      h.tensor_mul(out=bt.ap, in0=du_bf[m].ap,
                                                   in1=bbuf.ap),
                                      reads=[du_bf[m], bbuf], writes=[bt])
                        if 2 <= j:
                            for (k, (s2, mi)) in sg_items(j - 2):
                                for ds in range(2):
                                    ht = h_t[(k * 2 + ds) % 8]
                                    pd = pd_t[(k * 2 + ds) % 4]
                                    cbuf = ccs4[bix(s2) + ds]
                                    pr.op("vector", lambda h, pd=pd, ht=ht, cbuf=cbuf:
                                          h.tensor_mul(out=pd.ap, in0=ht.ap,
                                                       in1=cbuf.ap),
                                          reads=[ht, cbuf], writes=[pd])
                            for (k, (s2, mi)) in sg_items(j - 2):
                                dst = pdh_t[mi] if (s2 % 4) == 0 else ppd_t[k % 2]
                                pa = pd_t[(k * 2) % 4]; pb = pd_t[(k * 2 + 1) % 4]
                                pr.op("vector", lambda h, dst=dst, pa=pa, pb=pb:
                                      h.tensor_add(out=dst.h[:, 0:TC], in0=pa.ap,
                                                   in1=pb.ap),
                                      reads=[pa, pb], writes=[dst])
                            for (k, (s2, mi)) in sg_items(j - 2):
                                if (s2 % 4) == 0:
                                    continue
                                pp = ppd_t[k % 2]
                                ph = pdh_t[mi]
                                if (s2 % 8) == 2:
                                    p2 = pdh2_t[mi]
                                    pr.op("vector", lambda h, p2=p2, pp=pp, ph=ph:
                                          h.tensor_add(out=p2.ap, in0=pp.ap,
                                                       in1=ph.h[:, 0:TC]),
                                          reads=[ph, pp], writes=[p2])
                                else:
                                    pr.op("vector", lambda h, pp=pp, ph=ph:
                                          h.tensor_add(out=pp.ap, in0=pp.ap,
                                                       in1=ph.h[:, 0:TC]),
                                          reads=[ph, pp], writes=[pp])
                            for (k, (s2, mi)) in sg_items(j - 2):
                                if (s2 % 8) != 6:
                                    continue
                                pp = ppd_t[k % 2]
                                p2 = pdh2_t[mi]
                                pr.op("vector", lambda h, pp=pp, p2=p2: h.tensor_add(
                                    out=pp.ap, in0=pp.ap, in1=p2.ap),
                                    reads=[p2, pp], writes=[pp])
                            for (k, (s2, mi)) in sg_items(j - 2):
                                if (s2 % 8) != 6:
                                    continue
                                pp = ppd_t[k % 2]
                                ya = yac[mi]
                                pr.op("vector", lambda h, ya=ya, pp=pp: h.tensor_add(
                                    out=ya.ap, in0=ya.ap, in1=pp.ap),
                                    reads=[pp, ya], writes=[ya])
                            for (k, (s2, mi)) in sg_items(j - 2):
                                m = ms[mi]
                                for ds in range(2):
                                    ht = h_t[(k * 2 + ds) % 8]
                                    s = s2 + ds
                                    hc = hcol[l][m][s]
                                    col = m * S + s
                                    pr.op("scalar", lambda h, ht=ht, hc=hc, col=col:
                                          h.activation(out=hc.h[:, col:col + 1],
                                                       in_=ht.h[:, TC - 1:TC],
                                                       func=AF.Copy),
                                          reads=[ht], writes=[hc])
                        if 1 <= j:
                            for (k, (s2, mi)) in sg_items(j - 1):
                                m = ms[mi]
                                for ds in range(2):
                                    s = s2 + ds
                                    da = da_t[(k * 2 + ds) % 8]
                                    bt = b_t[(k * 2 + ds) % 8]
                                    ht = h_t[(k * 2 + ds) % 8]
                                    hc = hcol[l][m][s]
                                    col = m * S + s
                                    pr.op("vector",
                                          lambda h, ht=ht, da=da, bt=bt, hc=hc,
                                          col=col:
                                          h.tensor_tensor_scan(
                                              out=ht.ap, data0=da.ap, data1=bt.ap,
                                              initial=hc.h[:, col:col + 1],
                                              op0=ALU.mult, op1=ALU.add),
                                          reads=[da, bt, hc], writes=[ht])
                    def z_front(mi, m):
                        wt = wpool[m % 2]
                        pr.op("sync", lambda h, l=l, m=m, wt=wt: h.dma_start(
                            out=wt.h[:, 0:KD * 128],
                            in_=sc_in[l, KE + m].rearrange("p a b -> p (a b)")),
                            reads=[scb["sc_in"]], writes=[wt], dma=True)
                        ps = pool6[m % 6]
                        mm_acc(ps, lambda ps=ps: ps.ap,
                               [((lambda wt=wt, c=c: wt.h[:, 128 * c:128 * (c + 1)]),
                                 xin_bf[c], [wt]) for c in range(KD)])
                        zs = zs_t[mi % 2]
                        pr.op("scalar", lambda h, zs=zs, ps=ps: h.activation(
                            out=zs.ap, in_=ps.ap, func=AF.Silu),
                            reads=[ps], writes=[zs])
                    def z_back(mi, m):
                        ya = yac[mi]
                        zs = zs_t[mi % 2]
                        pr.op("vector", lambda h, ya=ya, zs=zs, m=m: h.tensor_mul(
                            out=du_bf[m].ap, in0=ya.ap, in1=zs.ap),
                            reads=[ya, zs], writes=[du_bf[m]])
                    for mi in range(10):
                        if mi >= 2:
                            z_back(mi - 2, ms[mi - 2])
                        if mi < 8:
                            z_front(mi, ms[mi])

                def op_front(k):
                    wt = wpool[k % 2]
                    pr.op("sync", lambda h, l=l, k=k, wt=wt: h.dma_start(
                        out=wt.h[:, 0:KE * 128],
                        in_=sc_out[l, k].rearrange("p a b -> p (a b)")),
                        reads=[scb["sc_out"]], writes=[wt], dma=True)
                    ps = pool6[k % 6]
                    mm_acc(ps, lambda ps=ps: ps.ap,
                           [((lambda wt=wt, c=c: wt.h[:, 128 * c:128 * (c + 1)]),
                             du_bf[c], [wt]) for c in range(KE)])
                def op_back(k):
                    ps = pool6[k % 6]
                    if l == 0:
                        pr.op("scalar", lambda h, ps=ps, k=k: h.activation(
                            out=xb[k].ap, in_=ps.ap, func=AF.Copy),
                            reads=[ps], writes=[xb[k]])
                    else:
                        pr.op("vector", lambda h, ps=ps, k=k: h.tensor_add(
                            out=xf[k].ap, in0=ps.ap, in1=xf[k].ap),
                            reads=[ps, xf[k]], writes=[xf[k]])
                for k in range(KD + 2):
                    if k < KD:
                        op_front(k)
                    if k >= 2:
                        op_back(k - 2)
            def layernorm(src_f, dst_f, dst_b):
                for k in range(KD):
                    pr.op("tensor", lambda h, k=k: h.matmul(
                        out=ps1.ap, lhsT=ones_col.ap, rhs=src_f[k].ap,
                        start=(k == 0), stop=(k == KD - 1)),
                        reads=[ones_col, src_f[k]], writes=[ps1])
                for k in range(KD):
                    sq = sq_t[k % 2]
                    pr.op("scalar", lambda h, k=k, sq=sq: h.activation(
                        out=sq.ap, in_=src_f[k].ap, func=AF.Square),
                        reads=[src_f[k]], writes=[sq])
                    pr.op("tensor", lambda h, k=k, sq=sq: h.matmul(
                        out=ps2.ap, lhsT=ones_col.ap, rhs=sq.ap,
                        start=(k == 0), stop=(k == KD - 1)),
                        reads=[ones_col, sq], writes=[ps2])
                pr.op("vector", lambda h: h.tensor_scalar_mul(
                    out=row_a.ap, in0=ps1.ap, scalar1=1.0 / D),
                    reads=[ps1], writes=[row_a])
                pr.op("vector", lambda h: h.tensor_mul(
                    out=row_b.ap, in0=row_a.ap, in1=row_a.ap),
                    reads=[row_a], writes=[row_b])
                pr.op("vector", lambda h: h.scalar_tensor_tensor(
                    out=row_b.ap, in0=ps2.ap, scalar=1.0 / D, in1=row_b.ap,
                    op0=ALU.mult, op1=ALU.subtract),
                    reads=[ps2, row_b], writes=[row_b])
                pr.op("scalar", lambda h: h.activation(
                    out=row_b.ap, in_=row_b.ap, func=AF.Ln, bias=epsrow.ap),
                    reads=[row_b, epsrow], writes=[row_b])
                pr.op("scalar", lambda h: h.activation(
                    out=row_b.ap, in_=row_b.ap, func=AF.Exp, scale=-0.5),
                    reads=[row_b], writes=[row_b])
                pr.op("vector", lambda h: h.tensor_mul(
                    out=row_c.ap, in0=row_a.ap, in1=row_b.ap),
                    reads=[row_a, row_b], writes=[row_c])
                pr.op("tensor", lambda h: h.matmul(
                    out=pbc[0].ap, lhsT=ones_row.ap, rhs=row_b.ap,
                    start=True, stop=True),
                    reads=[ones_row, row_b], writes=[pbc[0]])
                pr.op("tensor", lambda h: h.matmul(
                    out=pcc[0].ap, lhsT=ones_row.ap, rhs=row_c.ap,
                    start=True, stop=True),
                    reads=[ones_row, row_c], writes=[pcc[0]])
                for k in range(KD):
                    pr.op("vector", lambda h, k=k: h.tensor_mul(
                        out=dst_f[k].ap, in0=src_f[k].ap, in1=pbc[0].ap),
                        reads=[src_f[k], pbc[0]], writes=[dst_f[k]])
                    pr.op("vector", lambda h, k=k: h.tensor_sub(
                        out=dst_f[k].ap, in0=dst_f[k].ap, in1=pcc[0].ap),
                        reads=[dst_f[k], pcc[0]], writes=[dst_f[k]])
                    if dst_b is not None:
                        pr.op("scalar", lambda h, k=k: h.activation(
                            out=dst_b[k].ap, in_=dst_f[k].ap, func=AF.Copy),
                            reads=[dst_f[k]], writes=[dst_b[k]])

            layernorm(xf, xf, xb)

            hbt = u_bf + dt_bf[:KH - KE]
            def ffn_front(m):
                wt = wpool[m % 2]
                pr.op("sync", lambda h, m=m, wt=wt: h.dma_start(
                    out=wt.h[:, 0:2 * KD * 128],
                    in_=sc_gu[m].rearrange("p a b -> p (a b)")),
                    reads=[scb["sc_gu"]], writes=[wt], dma=True)
                psg = pool6[(2 * m) % 6]; psu = pool6[(2 * m + 1) % 6]
                mm_acc(psg, lambda psg=psg: psg.ap,
                       [((lambda wt=wt, c=c: wt.h[:, 128 * c:128 * (c + 1)]),
                         xb[c], [wt]) for c in range(KD)])
                mm_acc(psu, lambda psu=psu: psu.ap,
                       [((lambda wt=wt, c=c: wt.h[:, 128 * (KD + c):128 * (KD + c + 1)]),
                         xb[c], [wt]) for c in range(KD)])
                gs = zs_t[m % 2]
                pr.op("scalar", lambda h, gs=gs, psg=psg: h.activation(
                    out=gs.ap, in_=psg.ap, func=AF.Silu, scale=1.0 / 127.0),
                    reads=[psg], writes=[gs])
            def ffn_back(m):
                psu = pool6[(2 * m + 1) % 6]
                gs = zs_t[m % 2]
                pr.op("vector", lambda h, m=m, gs=gs, psu=psu: h.scalar_tensor_tensor(
                    out=hbt[m].ap, in0=psu.ap, scalar=1.0 / 127.0, in1=gs.ap,
                    op0=ALU.mult, op1=ALU.mult),
                    reads=[psu, gs], writes=[hbt[m]])
            for m in range(KH + 1):
                if m >= 1:
                    ffn_back(m - 1)
                if m < KH:
                    ffn_front(m)
            for k in range(KD):
                wta, wtb = wpool[0], wpool[1]
                pr.op("sync", lambda h, k=k, wta=wta: h.dma_start(
                    out=wta.h[:, 0:12 * 128],
                    in_=sc_d[k, :, 0:12, :].rearrange("p a b -> p (a b)")),
                    reads=[scb["sc_d"]], writes=[wta], dma=True)
                pr.op("sync", lambda h, k=k, wtb=wtb: h.dma_start(
                    out=wtb.h[:, 0:12 * 128],
                    in_=sc_d[k, :, 12:24, :].rearrange("p a b -> p (a b)")),
                    reads=[scb["sc_d"]], writes=[wtb], dma=True)
                ps = pool6[k % 6]
                mm_acc(ps, lambda ps=ps: ps.ap,
                       [((lambda wt=(wta if c < 12 else wtb), c=c:
                          wt.h[:, 128 * (c % 12):128 * (c % 12 + 1)]),
                         hbt[c], [wta if c < 12 else wtb]) for c in range(KH)])
                pr.op("vector", lambda h, ps=ps, k=k: h.scalar_tensor_tensor(
                    out=xf[k].ap, in0=ps.ap, scalar=1.0 / 127.0, in1=xf[k].ap,
                    op0=ALU.mult, op1=ALU.add),
                    reads=[ps, xf[k]], writes=[xf[k]])
            layernorm(xf, xf, None)

            for k in range(KD):
                pr.op("gpsimd", lambda h, k=k, t0=t0: h.dma_start(
                    out=outT[128 * k:128 * (k + 1), t0:t0 + TC], in_=xf[k].ap),
                    reads=[xf[k]], dma=True)

        pr.wait("gpsimd", pr.barrier_tok("dma_gpsimd"))
        pr.emit(nc, es)
    return nc



def _prep_inputs(L0_attn_x, L0_attn_in_proj_w, L0_attn_conv_w, L0_attn_conv_b,
                 L0_attn_x_proj_w, L0_attn_dt_proj_w, L0_attn_dt_proj_b,
                 L0_attn_A_log, L0_attn_D_skip, L0_attn_out_proj_w,
                 L0_ffn_w_gate, L0_ffn_w_up, L0_ffn_w_down):
    C = lambda a: np.ascontiguousarray(a)
    A = lambda a: np.asarray(a)
    shared = {
        "w_in": C(np.transpose(A(L0_attn_in_proj_w), (0, 2, 1))),
        "w_xp": C(np.transpose(A(L0_attn_x_proj_w), (0, 2, 1))),
        "w_dt": C(np.transpose(A(L0_attn_dt_proj_w), (0, 2, 1))),
        "w_out": C(np.transpose(A(L0_attn_out_proj_w), (0, 2, 1))),
        "cw_r": C(A(L0_attn_conv_w).reshape(NL, KE, 128, 4)
                  .transpose(0, 2, 1, 3).reshape(NL, 128, KE * 4)),
        "cb_r": C(A(L0_attn_conv_b).reshape(NL, KE, 128).transpose(0, 2, 1)),
        "dtb_r": C(A(L0_attn_dt_proj_b).reshape(NL, KE, 128).transpose(0, 2, 1)),
        "dsk_r": C(A(L0_attn_D_skip).reshape(NL, KE, 128).transpose(0, 2, 1)),
        "alog_r": C(A(L0_attn_A_log).reshape(NL, KE, 128, S)
                    .transpose(0, 2, 1, 3).reshape(NL, 128, KE * S)),
        "sel_all": np.repeat(np.eye(S, dtype=np.float32)[:, :, None]
                             .astype(ml_dtypes.bfloat16), 128, axis=2),
        "wg": C(A(L0_ffn_w_gate).T),
        "wu": C(A(L0_ffn_w_up).T),
        "wd": C(A(L0_ffn_w_down).T),
    }
    x = A(L0_attn_x)
    per_core = [dict(shared, xT=C(x[i].T)) for i in range(x.shape[0])]
    return per_core


def kernel(**inputs):
    per_core = _prep_inputs(**inputs)
    nc = build_core()
    res = run_bass_kernel_spmd(nc, per_core, core_ids=list(range(8)))
    outs = [np.ascontiguousarray(r["outT"].T) for r in res.results]
    return np.stack(outs, 0).astype(np.float32)
